# version 03
# baseline (speedup 1.0000x reference)
import numpy as np
import concourse.bass as bass
import concourse.mybir as mybir
from concourse.bass_utils import run_bass_kernel_spmd
from contextlib import ExitStack

F32 = mybir.dt.float32
BF16 = mybir.dt.bfloat16
I32 = mybir.dt.int32
AF = mybir.ActivationFunctionType
ALU = mybir.AluOpType
AX = mybir.AxisListType

D = 1024
S = 4096
NCORES = 8
MAGIC = 0x5F3759DF
EPS = 1e-5
AF_COPY = AF.Identity


class _Sem:
    __slots__ = ("h", "step", "n")

    def __init__(self, h, step):
        self.h = h
        self.step = step
        self.n = 0


class _Stream:
    __slots__ = ("e", "csem", "dsems", "drot", "waited")

    def __init__(self, e, csem, dsems):
        self.e = e
        self.csem = csem
        self.dsems = dsems
        self.drot = 0
        self.waited = {}


class _B:

    def __init__(self, nc, ctx):
        self.nc = nc
        self.ctx = ctx
        self.st = {}
        for name, eng, n_d in (
            ("pe", nc.tensor, 0),
            ("act", nc.scalar, 0),
            ("dve", nc.vector, 0),
            ("pool", nc.gpsimd, 6),
            ("sq", nc.sync, 6),
        ):
            cs = _Sem(ctx.enter_context(nc.semaphore(name + "_c")), 1)
            ds = [_Sem(ctx.enter_context(nc.semaphore(f"{name}_d{i}")), 16)
                  for i in range(n_d)]
            self.st[name] = _Stream(eng, cs, ds)

    def emit(self, stname, fn, deps=(), dma=False):
        st = self.st[stname]
        for dep in deps:
            if dep is None:
                continue
            sem, cnt = dep
            if st.waited.get(id(sem), 0) >= cnt:
                continue
            st.e.wait_ge(sem.h, cnt * sem.step)
            st.waited[id(sem)] = cnt
        if dma:
            sem = st.dsems[st.drot]
            st.drot = (st.drot + 1) % len(st.dsems)
        else:
            sem = st.csem
        h = fn(st.e)
        h.then_inc(sem.h, sem.step)
        sem.n += 1
        return (sem, sem.n)

    def wait(self, stname, deps):
        st = self.st[stname]
        for dep in deps:
            if dep is None:
                continue
            sem, cnt = dep
            if st.waited.get(id(sem), 0) >= cnt:
                continue
            st.e.wait_ge(sem.h, cnt * sem.step)
            st.waited[id(sem)] = cnt

    def snapshot(self):
        toks = []
        for st in self.st.values():
            if st.csem.n:
                toks.append((st.csem, st.csem.n))
            for ds in st.dsems:
                if ds.n:
                    toks.append((ds, ds.n))
        return toks

    def barrier(self, toks):
        for name, st in self.st.items():
            own = {id(st.csem)} | {id(d) for d in st.dsems}
            for sem, cnt in toks:
                if id(sem) in own:
                    continue
                self.wait(name, [(sem, cnt)])


def _re3(ap, k=128):
    return ap.rearrange("p (c k) -> p c k", k=k)


def build_program(nt, debug=False):
    own = (nt - 1) * 128
    rows = nt * 128
    nc = bass.Bass("TRN2", target_bir_lowering=False)
    dbg = {}
    if debug:
        dbg["dbg_x1"] = nc.dram_tensor("dbg_x1", [256, D], F32, kind="ExternalOutput").ap()

    xin = nc.dram_tensor("xin", [rows, D], F32, kind="ExternalInput").ap()
    kT = nc.dram_tensor("kT", [D, S], F32, kind="ExternalInput").ap()
    vv = nc.dram_tensor("vv", [S, D], F32, kind="ExternalInput").ap()
    wrT = nc.dram_tensor("wrT", [D, D], F32, kind="ExternalInput").ap()
    wkT = nc.dram_tensor("wkT", [D, D], F32, kind="ExternalInput").ap()
    wvT = nc.dram_tensor("wvT", [D, D], F32, kind="ExternalInput").ap()
    mur = nc.dram_tensor("mur", [128, 8], F32, kind="ExternalInput").ap()
    muk = nc.dram_tensor("muk", [128, 8], F32, kind="ExternalInput").ap()
    msk = nc.dram_tensor("msk", [128, 1], F32, kind="ExternalInput").ap()
    out = nc.dram_tensor("out", [own, D], F32, kind="ExternalOutput").ap()
    x1s = nc.dram_tensor("x1s", [rows, D], F32, kind="Internal").ap()

    with ExitStack() as ctx:
        b = _B(nc, ctx)
        sb = lambda n, s, d: ctx.enter_context(nc.sbuf_tensor(n, s, d)).ap()
        pst = lambda n, s, d: ctx.enter_context(nc.psum_tensor(n, s, d)).ap()

        ident = sb("ident", [128, 128], BF16)
        magic = sb("magic", [128, 1], I32)
        mur_sb = sb("mur_sb", [128, 8], F32)
        muk_sb = sb("muk_sb", [128, 8], F32)
        msk_sb = sb("msk_sb", [128, 1], F32)

        t_id0 = b.emit("pool", lambda e: e.memset(ident, 0.0))
        t_id = b.emit(
            "pool",
            lambda e: e.affine_select(
                out=ident, in_=ident, compare_op=ALU.not_equal, fill=1.0,
                base=0, pattern=[[-1, 128]], channel_multiplier=1),
            deps=[t_id0],
        )
        t_magic = b.emit("pool", lambda e: e.memset(magic, MAGIC))
        t_mur = b.emit("sq", lambda e: e.dma_start(out=mur_sb, in_=mur), dma=True)
        t_muk = b.emit("sq", lambda e: e.dma_start(out=muk_sb, in_=muk), dma=True)
        t_msk = b.emit("sq", lambda e: e.dma_start(out=msk_sb, in_=msk), dma=True)

        def rsqrt(stt, base, v_ap, vtok):
            c0 = stt[:, base:base + 1]
            y0 = stt[:, base + 1:base + 2]
            t1 = stt[:, base + 2:base + 3]
            t2 = stt[:, base + 3:base + 4]
            yf = stt[:, base + 4:base + 5]
            tk = b.emit("dve", lambda e: e.tensor_scalar(
                out=c0.bitcast(I32), in0=v_ap.bitcast(I32), scalar1=1,
                scalar2=None, op0=ALU.logical_shift_right), deps=[vtok])
            tk = b.emit("pool", lambda e: e.tensor_tensor(
                out=y0.bitcast(I32), in0=magic, in1=c0.bitcast(I32),
                op=ALU.subtract), deps=[tk, t_magic])
            tk = b.emit("dve", lambda e: e.tensor_tensor(out=t1, in0=y0, in1=y0, op=ALU.mult), deps=[tk])
            tk = b.emit("dve", lambda e: e.tensor_tensor(out=t2, in0=t1, in1=v_ap, op=ALU.mult), deps=[tk])
            tk = b.emit("dve", lambda e: e.tensor_scalar(out=t1, in0=t2, scalar1=-0.5, scalar2=1.5, op0=ALU.mult, op1=ALU.add), deps=[tk])
            tk = b.emit("dve", lambda e: e.tensor_tensor(out=yf, in0=y0, in1=t1, op=ALU.mult), deps=[tk])
            tk = b.emit("dve", lambda e: e.tensor_tensor(out=t2, in0=yf, in1=yf, op=ALU.mult), deps=[tk])
            tk = b.emit("dve", lambda e: e.tensor_tensor(out=t1, in0=t2, in1=v_ap, op=ALU.mult), deps=[tk])
            tk = b.emit("dve", lambda e: e.tensor_scalar(out=t2, in0=t1, scalar1=-0.5, scalar2=1.5, op0=ALU.mult, op1=ALU.add), deps=[tk])
            tk = b.emit("dve", lambda e: e.tensor_tensor(out=y0, in0=yf, in1=t2, op=ALU.mult), deps=[tk])
            return tk, y0

        with ExitStack() as actx:
            asb = lambda n, s, d: ctx_sb(actx, n, s, d)

            def ctx_sb(c, n, s, d):
                return c.enter_context(nc.sbuf_tensor(n, s, d)).ap()

            def ctx_ps(c, n, s, d):
                return c.enter_context(nc.psum_tensor(n, s, d)).ap()

            K_sb = ctx_sb(actx, "K_sb", [128, 8 * S], BF16)
            V_sb = ctx_sb(actx, "V_sb", [128, 32 * D], BF16)
            xb = [ctx_sb(actx, f"xb{i}", [128, D], F32) for i in range(2)]
            x1b = ctx_sb(actx, "x1b", [128, D], F32)
            ln1b = [ctx_sb(actx, f"ln1b{i}", [128, D], BF16) for i in range(2)]
            ln1T = [ctx_sb(actx, f"ln1T{i}", [128, D], BF16) for i in range(2)]
            gbf = [ctx_sb(actx, f"gbf{i}", [128, 512], BF16) for i in range(2)]
            gTb = [ctx_sb(actx, f"gTb{i}", [128, 512], BF16) for i in range(2)]
            ao = ctx_sb(actx, "ao", [128, D], F32)
            scr = ctx_sb(actx, "scr", [128, D], F32)
            junk = ctx_sb(actx, "junk", [128, D], F32)
            stats = [ctx_sb(actx, f"stats{i}", [128, 32], F32) for i in range(2)]

            ps_s = [ctx_ps(actx, f"ps_s{i}", [128, 512], F32) for i in range(2)]
            ps_t = [ctx_ps(actx, f"ps_t{i}", [128, 1024], BF16) for i in range(2)]
            ps_lt = [ctx_ps(actx, f"ps_lt{i}", [128, 1024], BF16) for i in range(2)]
            ps_o = ctx_ps(actx, "ps_o", [128, 1024], F32)

            kv3 = _re3(K_sb, k=S)
            kT3 = kT.rearrange("(c p) s -> p c s", p=128)
            t_kdma = []
            for j in range(8):
                sl = slice(j * 512, (j + 1) * 512)
                t_kdma.append(b.emit(
                    "pool",
                    lambda e, sl=sl: e.dma_start(out=kv3[:, :, sl], in_=kT3[:, :, sl]),
                    dma=True))
            t_vdma = []
            for g in range(8):
                src = vv[g * 512:(g + 1) * 512, :].rearrange("(c p) d -> p c d", p=128)
                dst = V_sb[:, g * 4096:(g + 1) * 4096].rearrange("p (c d) -> p c d", d=D)
                t_vdma.append(b.emit(
                    "pool", lambda e, src=src, dst=dst: e.dma_start(out=dst, in_=src),
                    dma=True))

            t_ld = {}

            def emit_load(t, dep):
                if t >= nt:
                    return
                p = t % 2
                t_ld[t] = b.emit(
                    "sq",
                    lambda e: e.dma_start(out=xb[p], in_=xin[t * 128:(t + 1) * 128, :]),
                    deps=[dep], dma=True)

            emit_load(0, None)
            emit_load(1, None)

            tok = {
                "ln1b_rd": [None, None],
                "ln1T_rd": [None, None],
                "pslt_rd": [None, None],
                "pss_rd": [None, None],
                "gbf_rd": [None, None],
                "pst_rd": [None, None],
                "gTb_rd": [None, None],
                "ao_wr": None,
                "ao_dve_rd": None,
                "tmp_wr": None,
                "x1add": {},
                "store": None,
                "mm2_last": None,
                "stats_rd_act": [None, None],
            }

            pdve = {}

            def pass_a_P_dve(t):
                p = t % 2
                stt = stats[p]
                d_x = t_ld[t]
                tk_r = b.emit("dve", lambda e: e.reduce_sum(
                    out=stt[:, 0:1], in_=xb[p], axis=AX.X), deps=[d_x])
                tk_m = b.emit("dve", lambda e: e.tensor_scalar(
                    out=stt[:, 1:2], in0=stt[:, 0:1], scalar1=1.0 / D,
                    scalar2=None, op0=ALU.mult), deps=[tk_r])
                tk_v = b.emit("dve", lambda e: e.scalar_tensor_tensor(
                    out=junk, in0=xb[p], scalar=stt[:, 1:2], in1=xb[p],
                    op0=ALU.subtract, op1=ALU.mult, accum_out=stt[:, 2:3]),
                    deps=[tk_m])
                tk_ve = b.emit("dve", lambda e: e.tensor_scalar(
                    out=stt[:, 3:4], in0=stt[:, 2:3], scalar1=1.0 / D,
                    scalar2=EPS, op0=ALU.mult, op1=ALU.add), deps=[tk_v])
                tk_rs, rstd = rsqrt(stt, 4, stt[:, 3:4], tk_ve)
                tk_b = b.emit("dve", lambda e: e.scalar_tensor_tensor(
                    out=stt[:, 9:10], in0=stt[:, 1:2], scalar=-1.0, in1=rstd,
                    op0=ALU.mult, op1=ALU.mult), deps=[tk_rs])
                pdve[t] = (tk_b, rstd)

            def pass_a_P_act(t):
                p = t % 2
                stt = stats[p]
                tk_b, rstd = pdve[t]
                t_n = b.emit("act", lambda e: e.activation(
                    out=ln1b[p], in_=xb[p], func=AF_COPY,
                    bias=stt[:, 9:10], scale=rstd),
                    deps=[tk_b, tok["ln1b_rd"][p], tok["stats_rd_act"][p]])
                tt = None
                for ci in range(8):
                    dl = [t_n, tok["pslt_rd"][p]] if ci == 0 else []
                    tt = b.emit("pe", lambda e, ci=ci: e.transpose(
                        ps_lt[p][:, ci * 128:(ci + 1) * 128],
                        ln1b[p][:, ci * 128:(ci + 1) * 128], ident),
                        deps=dl + ([t_id] if ci == 0 else []))
                tok["ln1b_rd"][p] = tt
                ev = b.emit("act", lambda e: e.copy(ln1T[p], ps_lt[p]),
                            deps=[tt, tok["ln1T_rd"][p]])
                tok["pslt_rd"][p] = ev
                return ev

            def pass_a_C(t, ev_ln1T):
                p = t % 2
                stt = stats[p]
                mm1 = {}
                gel = {}
                g2tok = {}
                tr = {}
                evg = {}
                mm2tok = None

                def do_mm1(j):
                    par = j % 2
                    tk = None
                    for d in range(8):
                        dl = []
                        if d == 0:
                            dl = [ev_ln1T, t_kdma[j], tok["pss_rd"][par]]
                        tk = b.emit("pe", lambda e, d=d, j=j, par=par: e.matmul(
                            ps_s[par],
                            ln1T[p][:, d * 128:(d + 1) * 128],
                            K_sb[:, d * S + j * 512: d * S + (j + 1) * 512],
                            start=(d == 0), stop=(d == 7)), deps=dl)
                    mm1[j] = tk

                def do_gelu(j):
                    par = j % 2
                    gel[j] = b.emit("act", lambda e, par=par: e.activation(
                        out=gbf[par], in_=ps_s[par], func=AF.Gelu),
                        deps=[mm1[j]] + list(tok["gbf_rd"][par] or ()))
                    tok["pss_rd"][par] = gel[j]

                def do_g2(j):
                    par = j % 2
                    g2tok[j] = b.emit("dve", lambda e, j=j, par=par: e.scalar_tensor_tensor(
                        out=junk[:, 0:512], in0=gbf[par], scalar=0.0,
                        in1=gbf[par], op0=ALU.bypass, op1=ALU.mult,
                        accum_out=stt[:, 21 + j:22 + j]), deps=[gel[j]])

                def do_tr(j):
                    par = j % 2
                    tk = None
                    for q in range(4):
                        dl = [gel[j], tok["pst_rd"][par]] if q == 0 else []
                        tk = b.emit("pe", lambda e, q=q, par=par: e.transpose(
                            ps_t[par][:, q * 128:(q + 1) * 128],
                            gbf[par][:, q * 128:(q + 1) * 128], ident), deps=dl)
                    tr[j] = tk

                def do_evg(j):
                    par = j % 2
                    evg[j] = b.emit("act", lambda e, par=par: e.copy(
                        gTb[par], ps_t[par][:, 0:512]),
                        deps=[tr[j], tok["gTb_rd"][par]])
                    tok["pst_rd"][par] = evg[j]

                def do_mm2(j):
                    nonlocal mm2tok
                    par = j % 2
                    tk = None
                    for q in range(4):
                        for h in range(2):
                            dl = []
                            if q == 0 and h == 0:
                                dl = [evg[j], t_vdma[j], tok["ao_wr"]]
                            st_flag = (j == 0 and q == 0)
                            sp_flag = (j == 7 and q == 3)
                            tk = b.emit("pe", lambda e, j=j, q=q, h=h, par=par,
                                        st_flag=st_flag, sp_flag=sp_flag: e.matmul(
                                ps_o[:, h * 512:(h + 1) * 512],
                                gTb[par][:, q * 128:(q + 1) * 128],
                                V_sb[:, (4 * j + q) * D + h * 512:(4 * j + q) * D + (h + 1) * 512],
                                start=st_flag, stop=sp_flag), deps=dl)
                    mm2[j] = tk
                    mm2tok = tk

                mm2 = {}
                do_mm1(0)
                do_gelu(0)
                do_mm1(1)
                do_gelu(1)
                if t + 1 < nt:
                    pass_a_P_dve(t + 1)
                do_g2(0)
                do_g2(1)
                for j in range(8):
                    do_tr(j)
                    do_evg(j)
                    tok["gbf_rd"][j % 2] = (tr[j], g2tok[j])
                    if j + 2 < 8:
                        do_mm1(j + 2)
                    do_mm2(j)
                    tok["gTb_rd"][j % 2] = mm2[j]
                    if j + 2 < 8:
                        do_gelu(j + 2)
                        do_g2(j + 2)
                tok["mm2_last"] = mm2tok
                tok["ln1T_rd"][p] = mm1[7]
                ev_next = pass_a_P_act(t + 1) if t + 1 < nt else None
                return g2tok[7], mm2tok, ev_next

            def pass_a_E(t, g2last, mm2tok):
                p = t % 2
                stt = stats[p]
                t_ao = b.emit("act", lambda e: e.copy(ao, ps_o),
                              deps=[mm2tok, tok["ao_dve_rd"]])
                tok["ao_wr"] = t_ao
                t_sg = b.emit("dve", lambda e: e.reduce_sum(
                    out=stt[:, 29:30], in_=stt[:, 21:29], axis=AX.X),
                    deps=[g2last])
                tk_r = b.emit("dve", lambda e: e.reduce_sum(
                    out=stt[:, 10:11], in_=ao, axis=AX.X), deps=[t_ao])
                tk_m = b.emit("dve", lambda e: e.tensor_scalar(
                    out=stt[:, 11:12], in0=stt[:, 10:11], scalar1=1.0 / D,
                    scalar2=None, op0=ALU.mult), deps=[tk_r])
                tk_v = b.emit("dve", lambda e: e.scalar_tensor_tensor(
                    out=junk, in0=ao, scalar=stt[:, 11:12], in1=ao,
                    op0=ALU.subtract, op1=ALU.mult, accum_out=stt[:, 12:13]),
                    deps=[tk_m])
                tok["ao_dve_rd"] = tk_v
                tk_vs = b.emit("dve", lambda e: e.tensor_scalar(
                    out=stt[:, 13:14], in0=stt[:, 12:13], scalar1=1.0 / D,
                    scalar2=None, op0=ALU.mult), deps=[tk_v])
                tk_ve = b.emit("dve", lambda e: e.scalar_tensor_tensor(
                    out=stt[:, 14:15], in0=stt[:, 29:30], scalar=EPS / S,
                    in1=stt[:, 13:14], op0=ALU.mult, op1=ALU.add),
                    deps=[tk_vs, t_sg])
                tk_rs, rstd = rsqrt(stt, 15, stt[:, 14:15], tk_ve)
                tk_b = b.emit("dve", lambda e: e.scalar_tensor_tensor(
                    out=stt[:, 20:21], in0=stt[:, 11:12], scalar=-1.0, in1=rstd,
                    op0=ALU.mult, op1=ALU.mult), deps=[tk_rs])
                tok["stats_rd_act"][p] = None
                t_tmp = b.emit("act", lambda e: e.activation(
                    out=scr, in_=ao, func=AF_COPY, bias=stt[:, 20:21],
                    scale=rstd), deps=[tk_b, tok.get("x1add_prev_scr")])
                tok["stats_rd_act"][p] = t_tmp
                t_x1 = b.emit("dve", lambda e: e.tensor_tensor(
                    out=x1b, in0=xb[p], in1=scr, op=ALU.add),
                    deps=[t_tmp, tok["store"]])
                tok["x1add_prev_scr"] = t_x1
                tok["x1add"][t] = t_x1
                tok["store"] = b.emit("sq", lambda e: e.dma_start(
                    out=x1s[t * 128:(t + 1) * 128, :], in_=x1b),
                    deps=[t_x1], dma=True)

            tok["x1add_prev_scr"] = None
            pass_a_P_dve(0)
            ev_next = pass_a_P_act(0)
            g2_prev = mm2_prev = None
            for t in range(nt):
                if t > 0:
                    pass_a_E(t - 1, g2_prev, mm2_prev)
                    emit_load(t + 1, tok["x1add"].get(t - 1))
                g2_prev, mm2_prev, ev_next = pass_a_C(t, ev_next)
            pass_a_E(nt - 1, g2_prev, mm2_prev)

            bar = b.snapshot()

        b.barrier(bar)

        with ExitStack() as bctx:
            def bsb(n, s, d):
                return bctx.enter_context(nc.sbuf_tensor(n, s, d)).ap()

            def bps(n, s, d):
                return bctx.enter_context(nc.psum_tensor(n, s, d)).ap()

            Wr_sb = bsb("Wr_sb", [128, 8 * D], BF16)
            Wk_sb = bsb("Wk_sb", [128, 8 * D], BF16)
            Wv_sb = bsb("Wv_sb", [128, 8 * D], BF16)
            y1b = [bsb(f"y1b{i}", [128, D], F32) for i in range(2)]
            ln2b = [bsb(f"ln2b{i}", [128, D], BF16) for i in range(2)]
            ln2T = [bsb(f"ln2T{i}", [128, D], BF16) for i in range(2)]
            sh = bsb("sh", [128, D], BF16)
            xrT = bsb("xrT", [128, D], BF16)
            xkT = bsb("xkT", [128, D], BF16)
            mixd = bsb("mixd", [128, D], F32)
            t_rb = bsb("t_rb", [128, D], F32)
            scrB = bsb("scrB", [128, D], F32)
            kctb = bsb("kctb", [128, D], BF16)
            kctT = bsb("kctT", [128, D], BF16)
            yb = bsb("yb", [128, D], F32)
            outb = bsb("outb", [128, D], F32)
            statsB = [bsb(f"statsB{i}", [128, 32], F32) for i in range(2)]
            prevc = [bsb(f"prevc{i}", [128, 8], BF16) for i in range(2)]

            ps_r = bps("ps_r", [128, 1024], F32)
            ps_k = bps("ps_k", [128, 1024], F32)
            ps_v = bps("ps_v", [128, 1024], F32)
            ps_t1 = bps("ps_tb1", [128, 1024], BF16)
            ps_t2 = bps("ps_tb2", [128, 1024], BF16)

            t_wdma = {"r": [], "k": [], "v": []}
            for nm, dram, dst in (("r", wrT, Wr_sb), ("k", wkT, Wk_sb), ("v", wvT, Wv_sb)):
                for h in range(2):
                    src = dram[h * 512:(h + 1) * 512, :].rearrange("(c p) d -> p c d", p=128)
                    dsv = dst[:, h * 4096:(h + 1) * 4096].rearrange("p (c d) -> p c d", d=D)
                    t_wdma[nm].append(b.emit(
                        "pool",
                        lambda e, src=src, dsv=dsv: e.dma_start(out=dsv, in_=src),
                        dma=True))
            t_pc0 = b.emit("pool", lambda e: e.memset(prevc[0], 0.0))
            t_pc1 = b.emit("pool", lambda e: e.memset(prevc[1], 0.0))

            t_yld = {}

            def emit_yload(t, dep):
                if t >= nt:
                    return
                p = t % 2
                t_yld[t] = b.emit(
                    "sq",
                    lambda e: e.dma_start(out=y1b[p], in_=x1s[t * 128:(t + 1) * 128, :]),
                    deps=[dep], dma=True)

            emit_yload(0, None)
            emit_yload(1, None)

            tokB = {
                "ln2b_rd": [None, None],
                "ln2T_rd": [None, None],
                "pst1_rd": None,
                "pst2_rd": None,
                "sh_rd": None,
                "xr_rd": None, "xk_rd": None,
                "psr_rd": None, "psk_rd": None, "psv_rd": None,
                "tr_rd": None,
                "scrB_rd": None,
                "kctb_rd": None, "kctT_rd": None,
                "yb_rd": None,
                "outb_st": None,
                "y1_rd": {},
                "prevc_rd": [t_pc0, t_pc1],
                "prevc_wr": [t_pc0, t_pc1],
                "stats_rd_act": [None, None],
                "mixd_rd": None,
            }

            pdveB = {}

            def stage1_dve(t):
                p = t % 2
                stt = statsB[p]
                d_y = t_yld[t]
                tk_r = b.emit("dve", lambda e: e.reduce_sum(
                    out=stt[:, 0:1], in_=y1b[p], axis=AX.X), deps=[d_y])
                tk_m = b.emit("dve", lambda e: e.tensor_scalar(
                    out=stt[:, 1:2], in0=stt[:, 0:1], scalar1=1.0 / D,
                    scalar2=None, op0=ALU.mult), deps=[tk_r])
                tk_v = b.emit("dve", lambda e: e.scalar_tensor_tensor(
                    out=mixd, in0=y1b[p], scalar=stt[:, 1:2], in1=y1b[p],
                    op0=ALU.subtract, op1=ALU.mult, accum_out=stt[:, 2:3]),
                    deps=[tk_m, tokB["mixd_rd"]])
                tokB["mixd_rd"] = tk_v
                tk_ve = b.emit("dve", lambda e: e.tensor_scalar(
                    out=stt[:, 3:4], in0=stt[:, 2:3], scalar1=1.0 / D,
                    scalar2=EPS, op0=ALU.mult, op1=ALU.add), deps=[tk_v])
                tk_rs, rstd = rsqrt(stt, 4, stt[:, 3:4], tk_ve)
                tk_b = b.emit("dve", lambda e: e.scalar_tensor_tensor(
                    out=stt[:, 9:10], in0=stt[:, 1:2], scalar=-1.0, in1=rstd,
                    op0=ALU.mult, op1=ALU.mult), deps=[tk_rs])
                pdveB[t] = (tk_b, rstd)

            def stage1_ln2b(t):
                p = t % 2
                stt = statsB[p]
                tk_b, rstd = pdveB[t]
                return b.emit("act", lambda e: e.activation(
                    out=ln2b[p], in_=y1b[p], func=AF_COPY, bias=stt[:, 9:10],
                    scale=rstd), deps=[tk_b, tokB["ln2b_rd"][p], tokB["stats_rd_act"][p]])

            def stage1_tr(t, t_n):
                p = t % 2
                tt = None
                for ci in range(8):
                    dl = [t_n, tokB["pst1_rd"], t_id] if ci == 0 else []
                    tt = b.emit("pe", lambda e, ci=ci: e.transpose(
                        ps_t1[:, ci * 128:(ci + 1) * 128],
                        ln2b[p][:, ci * 128:(ci + 1) * 128], ident), deps=dl)
                tokB["ln2b_rd"][p] = tt
                ev = b.emit("act", lambda e: e.copy(ln2T[p], ps_t1),
                            deps=[tt, tokB["ln2T_rd"][p]])
                tokB["pst1_rd"] = ev
                l3 = _re3(ln2T[p])
                pcw = b.emit("dve", lambda e: e.tensor_copy(
                    out=_re3(prevc[p], k=1), in_=l3[:, :, 127:128]),
                    deps=[ev, tokB["prevc_rd"][p]])
                tokB["prevc_wr"][p] = pcw
                return ev

            def stage1_shmix(t, ev_ln2T):
                p = t % 2
                l3 = _re3(ln2T[p])
                s3 = _re3(sh)
                if t == 0:
                    c_bulk = b.emit("dve", lambda e: e.tensor_copy(
                        out=s3[:, :, 2:128], in_=l3[:, :, 1:127]),
                        deps=[ev_ln2T, tokB["sh_rd"]])
                    c_m = b.emit("dve", lambda e: e.tensor_scalar(
                        out=s3[:, :, 1:2], in0=l3[:, :, 0:1],
                        scalar1=msk_sb[:, 0:1], scalar2=None, op0=ALU.mult),
                        deps=[ev_ln2T, t_msk])
                    c_0 = b.emit("dve", lambda e: e.tensor_copy(
                        out=s3[:, :, 0:1], in_=_re3(prevc[1], k=1)),
                        deps=[tokB["prevc_wr"][1]])
                    tokB["prevc_rd"][1] = c_0
                    sh_deps = [c_bulk, c_m, c_0]
                else:
                    c_bulk = b.emit("dve", lambda e: e.tensor_copy(
                        out=s3[:, :, 1:128], in_=l3[:, :, 0:127]),
                        deps=[ev_ln2T, tokB["sh_rd"]])
                    pp = (t - 1) % 2
                    c_0 = b.emit("dve", lambda e: e.tensor_copy(
                        out=s3[:, :, 0:1], in_=_re3(prevc[pp], k=1)),
                        deps=[tokB["prevc_wr"][pp]])
                    tokB["prevc_rd"][pp] = c_0
                    sh_deps = [c_bulk, c_0]
                xr_toks = []
                xk_toks = []
                last_mix = None
                for ci in range(8):
                    cs = slice(ci * 128, (ci + 1) * 128)
                    dl = sh_deps + [tokB["xr_rd"], tokB["xk_rd"], tokB["mixd_rd"]] if ci == 0 else []
                    td = b.emit("dve", lambda e, cs=cs: e.tensor_tensor(
                        out=mixd[:, cs], in0=ln2T[p][:, cs], in1=sh[:, cs],
                        op=ALU.subtract), deps=dl)
                    tr_ = b.emit("dve", lambda e, cs=cs, ci=ci: e.scalar_tensor_tensor(
                        out=xrT[:, cs], in0=mixd[:, cs], scalar=mur_sb[:, ci:ci + 1],
                        in1=sh[:, cs], op0=ALU.mult, op1=ALU.add),
                        deps=[td] + ([t_mur] if ci == 0 else []))
                    tk_ = b.emit("dve", lambda e, cs=cs, ci=ci: e.scalar_tensor_tensor(
                        out=xkT[:, cs], in0=mixd[:, cs], scalar=muk_sb[:, ci:ci + 1],
                        in1=sh[:, cs], op0=ALU.mult, op1=ALU.add),
                        deps=[td] + ([t_muk] if ci == 0 else []))
                    xr_toks.append(tr_)
                    xk_toks.append(tk_)
                    last_mix = tk_
                tokB["sh_rd"] = last_mix
                tokB["ln2T_rd"][p] = last_mix
                tokB["mixd_rd"] = last_mix
                return xr_toks, xk_toks, last_mix

            def ffn_mm(ps, xT, W_sb, chunk_toks, wdmas, free_tok):
                tk = None
                for ci in range(8):
                    for h in range(2):
                        dl = [chunk_toks[ci], wdmas[ci // 4]]
                        if ci == 0 and h == 0:
                            dl.append(free_tok)
                        tk = b.emit("pe", lambda e, ci=ci, h=h, ps=ps, xT=xT, W_sb=W_sb: e.matmul(
                            ps[:, h * 512:(h + 1) * 512],
                            xT[:, ci * 128:(ci + 1) * 128],
                            W_sb[:, ci * D + h * 512: ci * D + (h + 1) * 512],
                            start=(ci == 0), stop=(ci == 7)), deps=dl)
                return tk

            def stage_eps(t, t_y):
                p = t % 2
                stt = statsB[p]
                tk_r = b.emit("dve", lambda e: e.reduce_sum(
                    out=stt[:, 10:11], in_=yb, axis=AX.X), deps=[t_y])
                tk_m = b.emit("dve", lambda e: e.tensor_scalar(
                    out=stt[:, 11:12], in0=stt[:, 10:11], scalar1=1.0 / D,
                    scalar2=None, op0=ALU.mult), deps=[tk_r])
                tk_v = b.emit("dve", lambda e: e.scalar_tensor_tensor(
                    out=mixd, in0=yb, scalar=stt[:, 11:12], in1=yb,
                    op0=ALU.subtract, op1=ALU.mult, accum_out=stt[:, 12:13]),
                    deps=[tk_m, tokB["mixd_rd"]])
                tokB["mixd_rd"] = tk_v
                tk_ve = b.emit("dve", lambda e: e.tensor_scalar(
                    out=stt[:, 13:14], in0=stt[:, 12:13], scalar1=1.0 / D,
                    scalar2=4 * EPS, op0=ALU.mult, op1=ALU.add), deps=[tk_v])
                tk_rs, rstd = rsqrt(stt, 14, stt[:, 13:14], tk_ve)
                tk_b = b.emit("dve", lambda e: e.scalar_tensor_tensor(
                    out=stt[:, 19:20], in0=stt[:, 11:12], scalar=-1.0, in1=rstd,
                    op0=ALU.mult, op1=ALU.mult), deps=[tk_rs])
                t_tmp = b.emit("act", lambda e: e.activation(
                    out=scrB, in_=yb, func=AF_COPY, bias=stt[:, 19:20],
                    scale=rstd), deps=[tk_b, tokB["scrB_rd"]])
                tokB["stats_rd_act"][p] = t_tmp
                t_o = b.emit("dve", lambda e: e.tensor_tensor(
                    out=outb, in0=y1b[p], in1=scrB, op=ALU.add),
                    deps=[t_tmp, tokB["outb_st"]])
                tokB["scrB_rd"] = t_o
                tokB["yb_rd"] = (tk_v, t_tmp)
                tokB["y1_rd"][t] = t_o
                vs = max(0, t * 128 - 1)
                ve = min(own - 1, t * 128 + 126)
                if ve >= vs:
                    rs = vs - (t * 128 - 1)
                    cnt = ve - vs + 1
                    tokB["outb_st"] = b.emit("sq", lambda e: e.dma_start(
                        out=out[vs:vs + cnt, :], in_=outb[rs:rs + cnt, :]),
                        deps=[t_o], dma=True)

            stage1_dve(0)
            ev0 = stage1_tr(0, stage1_ln2b(0))
            mix_cur = stage1_shmix(0, ev0)
            for t in range(nt):
                p = t % 2
                xr_toks, xk_toks, last_mix = mix_cur
                mr = ffn_mm(ps_r, xrT, Wr_sb, xr_toks, t_wdma["r"], tokB["psr_rd"])
                tokB["xr_rd"] = mr
                t_t = b.emit("act", lambda e: e.activation(
                    out=t_rb, in_=ps_r, func=AF.Tanh, scale=0.5),
                    deps=[mr, tokB["tr_rd"]])
                tokB["psr_rd"] = t_t
                if t + 1 < nt:
                    stage1_dve(t + 1)
                mk = ffn_mm(ps_k, xkT, Wk_sb, xk_toks, t_wdma["k"], tokB["psk_rd"])
                tokB["xk_rd"] = mk
                t_kr = b.emit("act", lambda e: e.activation(
                    out=scrB, in_=ps_k, func=AF.Relu),
                    deps=[mk, tokB["scrB_rd"]])
                tokB["psk_rd"] = t_kr
                t_ka = b.emit("act", lambda e: e.activation(
                    out=kctb, in_=scrB, func=AF.Square),
                    deps=[t_kr, tokB["kctb_rd"]])
                tokB["scrB_rd"] = t_ka
                ev_next = None
                if t + 1 < nt:
                    ev_next = stage1_tr(t + 1, stage1_ln2b(t + 1))
                tt = None
                for ci in range(8):
                    dl = [t_ka, tokB["pst2_rd"]] if ci == 0 else []
                    tt = b.emit("pe", lambda e, ci=ci: e.transpose(
                        ps_t2[:, ci * 128:(ci + 1) * 128],
                        kctb[:, ci * 128:(ci + 1) * 128], ident), deps=dl)
                tokB["kctb_rd"] = tt
                ev2 = b.emit("act", lambda e: e.copy(kctT, ps_t2),
                             deps=[tt, tokB["kctT_rd"]])
                tokB["pst2_rd"] = ev2
                mv = ffn_mm(ps_v, kctT, Wv_sb, [ev2] * 8, t_wdma["v"], tokB["psv_rd"])
                tokB["kctT_rd"] = mv
                if t + 1 < nt:
                    mix_cur = stage1_shmix(t + 1, ev_next)
                t_s = b.emit("dve", lambda e: e.tensor_tensor(
                    out=mixd, in0=t_rb, in1=ps_v, op=ALU.mult),
                    deps=[mv, t_t, tokB["mixd_rd"]])
                tokB["tr_rd"] = t_s
                tokB["mixd_rd"] = t_s
                t_y = b.emit("dve", lambda e: e.tensor_tensor(
                    out=yb, in0=mixd, in1=ps_v, op=ALU.add),
                    deps=[t_s] + list(tokB["yb_rd"] or ()))
                tokB["psv_rd"] = t_y
                stage_eps(t, t_y)
                emit_yload(t + 2, tokB["y1_rd"].get(t))

            if debug:
                b.emit("sq", lambda e: e.dma_start(
                    out=dbg["dbg_x1"], in_=x1s[0:256, :]), dma=True)
            sq = b.st["sq"]
            b.wait("sq", [(ds, ds.n) for ds in sq.dsems if ds.n])

    return nc


def _prep_inputs(x_flat, kT, v, wrT, wkT, wvT, mur, muk, nt=33):
    own = (nt - 1) * 128
    maps = []
    for c in range(NCORES):
        lo = c * own
        sl = x_flat[lo:lo + own]
        halo = x_flat[lo - 1:lo] if (lo % 8192) != 0 else sl[0:1]
        mask = np.ones((128, 1), np.float32) if (lo % 8192) != 0 else np.zeros((128, 1), np.float32)
        pad = sl[0:127]
        xin = np.ascontiguousarray(np.concatenate([halo, sl, pad], axis=0))
        maps.append({
            "xin": xin, "kT": kT, "vv": v, "wrT": wrT, "wkT": wkT,
            "wvT": wvT, "mur": mur, "muk": muk, "msk": mask,
        })
    return maps


def kernel(L0_attn_inputs, L0_attn_key_param_tokens, L0_attn_value_param_tokens,
           L0_ffn_W_r, L0_ffn_W_k, L0_ffn_W_v, L0_ffn_mu_r, L0_ffn_mu_k):
    x = np.ascontiguousarray(L0_attn_inputs, dtype=np.float32).reshape(-1, D)
    kT = np.ascontiguousarray(np.asarray(L0_attn_key_param_tokens, dtype=np.float32).T)
    v = np.ascontiguousarray(L0_attn_value_param_tokens, dtype=np.float32)
    wrT = np.ascontiguousarray(np.asarray(L0_ffn_W_r, dtype=np.float32).T)
    wkT = np.ascontiguousarray(np.asarray(L0_ffn_W_k, dtype=np.float32).T)
    wvT = np.ascontiguousarray(np.asarray(L0_ffn_W_v, dtype=np.float32).T)
    mur = np.ascontiguousarray(np.asarray(L0_ffn_mu_r, dtype=np.float32).reshape(8, 128).T)
    muk = np.ascontiguousarray(np.asarray(L0_ffn_mu_k, dtype=np.float32).reshape(8, 128).T)

    nt = 33
    nc = build_program(nt)
    in_maps = _prep_inputs(x, kT, v, wrT, wkT, wvT, mur, muk, nt)
    res = run_bass_kernel_spmd(nc, in_maps, core_ids=list(range(NCORES)))
    outs = [res.results[c]["out"] for c in range(NCORES)]
    full = np.concatenate(outs, axis=0).reshape(4, 8192, D)
    return full


# version 04
# speedup vs baseline: 1.3137x; 1.3137x over previous
import numpy as np
import concourse.bass as bass
import concourse.mybir as mybir
from concourse.bass_utils import run_bass_kernel_spmd
from contextlib import ExitStack

F32 = mybir.dt.float32
BF16 = mybir.dt.bfloat16
I32 = mybir.dt.int32
AF = mybir.ActivationFunctionType
ALU = mybir.AluOpType
AX = mybir.AxisListType

D = 1024
S = 4096
NCORES = 8
MAGIC = 0x5F3759DF
EPS = 1e-5
AF_COPY = AF.Identity


class _Sem:
    __slots__ = ("h", "step", "n")

    def __init__(self, h, step):
        self.h = h
        self.step = step
        self.n = 0


class _Stream:
    __slots__ = ("e", "csem", "dsems", "drot", "waited")

    def __init__(self, e, csem, dsems):
        self.e = e
        self.csem = csem
        self.dsems = dsems
        self.drot = 0
        self.waited = {}


class _B:

    def __init__(self, nc, ctx):
        self.nc = nc
        self.ctx = ctx
        self.st = {}
        for name, eng, n_d in (
            ("pe", nc.tensor, 0),
            ("act", nc.scalar, 0),
            ("dve", nc.vector, 0),
            ("pool", nc.gpsimd, 6),
            ("sq", nc.sync, 6),
        ):
            cs = _Sem(ctx.enter_context(nc.semaphore(name + "_c")), 1)
            ds = [_Sem(ctx.enter_context(nc.semaphore(f"{name}_d{i}")), 16)
                  for i in range(n_d)]
            self.st[name] = _Stream(eng, cs, ds)

    def emit(self, stname, fn, deps=(), dma=False):
        st = self.st[stname]
        for dep in deps:
            if dep is None:
                continue
            sem, cnt = dep
            if st.waited.get(id(sem), 0) >= cnt:
                continue
            st.e.wait_ge(sem.h, cnt * sem.step)
            st.waited[id(sem)] = cnt
        if dma:
            sem = st.dsems[st.drot]
            st.drot = (st.drot + 1) % len(st.dsems)
        else:
            sem = st.csem
        h = fn(st.e)
        h.then_inc(sem.h, sem.step)
        sem.n += 1
        return (sem, sem.n)

    def wait(self, stname, deps):
        st = self.st[stname]
        for dep in deps:
            if dep is None:
                continue
            sem, cnt = dep
            if st.waited.get(id(sem), 0) >= cnt:
                continue
            st.e.wait_ge(sem.h, cnt * sem.step)
            st.waited[id(sem)] = cnt

    def snapshot(self):
        toks = []
        for st in self.st.values():
            if st.csem.n:
                toks.append((st.csem, st.csem.n))
            for ds in st.dsems:
                if ds.n:
                    toks.append((ds, ds.n))
        return toks

    def barrier(self, toks):
        for name, st in self.st.items():
            own = {id(st.csem)} | {id(d) for d in st.dsems}
            for sem, cnt in toks:
                if id(sem) in own:
                    continue
                self.wait(name, [(sem, cnt)])


def _re3(ap, k=128):
    return ap.rearrange("p (c k) -> p c k", k=k)


def build_program(nt, debug=False):
    own = (nt - 1) * 128
    rows = nt * 128
    nc = bass.Bass("TRN2", target_bir_lowering=False)
    dbg = {}
    if debug:
        dbg["dbg_x1"] = nc.dram_tensor("dbg_x1", [256, D], F32, kind="ExternalOutput").ap()

    xin = nc.dram_tensor("xin", [rows, D], F32, kind="ExternalInput").ap()
    kT = nc.dram_tensor("kT", [D, S], F32, kind="ExternalInput").ap()
    vv = nc.dram_tensor("vv", [S, D], F32, kind="ExternalInput").ap()
    wrT = nc.dram_tensor("wrT", [D, D], F32, kind="ExternalInput").ap()
    wkT = nc.dram_tensor("wkT", [D, D], F32, kind="ExternalInput").ap()
    wvT = nc.dram_tensor("wvT", [D, D], F32, kind="ExternalInput").ap()
    mur = nc.dram_tensor("mur", [128, 8], F32, kind="ExternalInput").ap()
    muk = nc.dram_tensor("muk", [128, 8], F32, kind="ExternalInput").ap()
    msk = nc.dram_tensor("msk", [128, 1], F32, kind="ExternalInput").ap()
    out = nc.dram_tensor("out", [own, D], F32, kind="ExternalOutput").ap()
    x1s = nc.dram_tensor("x1s", [rows, D], F32, kind="Internal").ap()

    with ExitStack() as ctx:
        b = _B(nc, ctx)
        sb = lambda n, s, d: ctx.enter_context(nc.sbuf_tensor(n, s, d)).ap()
        pst = lambda n, s, d: ctx.enter_context(nc.psum_tensor(n, s, d)).ap()

        ident = sb("ident", [128, 128], BF16)
        magic = sb("magic", [128, 1], I32)
        mur_sb = sb("mur_sb", [128, 8], F32)
        muk_sb = sb("muk_sb", [128, 8], F32)
        msk_sb = sb("msk_sb", [128, 1], F32)

        t_id0 = b.emit("pool", lambda e: e.memset(ident, 0.0))
        t_id = b.emit(
            "pool",
            lambda e: e.affine_select(
                out=ident, in_=ident, compare_op=ALU.not_equal, fill=1.0,
                base=0, pattern=[[-1, 128]], channel_multiplier=1),
            deps=[t_id0],
        )
        t_magic = b.emit("pool", lambda e: e.memset(magic, MAGIC))
        t_mur = b.emit("sq", lambda e: e.dma_start(out=mur_sb, in_=mur), dma=True)
        t_muk = b.emit("sq", lambda e: e.dma_start(out=muk_sb, in_=muk), dma=True)
        t_msk = b.emit("sq", lambda e: e.dma_start(out=msk_sb, in_=msk), dma=True)

        def rsqrt(stt, base, v_ap, vtok):
            c0 = stt[:, base:base + 1]
            y0 = stt[:, base + 1:base + 2]
            t1 = stt[:, base + 2:base + 3]
            t2 = stt[:, base + 3:base + 4]
            yf = stt[:, base + 4:base + 5]
            tk = b.emit("dve", lambda e: e.tensor_scalar(
                out=c0.bitcast(I32), in0=v_ap.bitcast(I32), scalar1=1,
                scalar2=None, op0=ALU.logical_shift_right), deps=[vtok])
            tk = b.emit("pool", lambda e: e.tensor_tensor(
                out=y0.bitcast(I32), in0=magic, in1=c0.bitcast(I32),
                op=ALU.subtract), deps=[tk, t_magic])
            tk = b.emit("dve", lambda e: e.tensor_tensor(out=t1, in0=y0, in1=y0, op=ALU.mult), deps=[tk])
            tk = b.emit("dve", lambda e: e.tensor_tensor(out=t2, in0=t1, in1=v_ap, op=ALU.mult), deps=[tk])
            tk = b.emit("dve", lambda e: e.tensor_scalar(out=t1, in0=t2, scalar1=-0.5, scalar2=1.5, op0=ALU.mult, op1=ALU.add), deps=[tk])
            tk = b.emit("dve", lambda e: e.tensor_tensor(out=yf, in0=y0, in1=t1, op=ALU.mult), deps=[tk])
            tk = b.emit("dve", lambda e: e.tensor_tensor(out=t2, in0=yf, in1=yf, op=ALU.mult), deps=[tk])
            tk = b.emit("dve", lambda e: e.tensor_tensor(out=t1, in0=t2, in1=v_ap, op=ALU.mult), deps=[tk])
            tk = b.emit("dve", lambda e: e.tensor_scalar(out=t2, in0=t1, scalar1=-0.5, scalar2=1.5, op0=ALU.mult, op1=ALU.add), deps=[tk])
            tk = b.emit("dve", lambda e: e.tensor_tensor(out=y0, in0=yf, in1=t2, op=ALU.mult), deps=[tk])
            return tk, y0

        with ExitStack() as actx:
            asb = lambda n, s, d: ctx_sb(actx, n, s, d)

            def ctx_sb(c, n, s, d):
                return c.enter_context(nc.sbuf_tensor(n, s, d)).ap()

            def ctx_ps(c, n, s, d):
                return c.enter_context(nc.psum_tensor(n, s, d)).ap()

            K_sb = ctx_sb(actx, "K_sb", [128, 8 * S], BF16)
            V_sb = ctx_sb(actx, "V_sb", [128, 32 * D], BF16)
            xb = [ctx_sb(actx, f"xb{i}", [128, D], F32) for i in range(3)]
            x1b = ctx_sb(actx, "x1b", [128, D], F32)
            ln1b = [ctx_sb(actx, f"ln1b{i}", [128, D], BF16) for i in range(2)]
            ln1T = [ctx_sb(actx, f"ln1T{i}", [128, D], BF16) for i in range(2)]
            gbf = [ctx_sb(actx, f"gbf{i}", [128, 512], BF16) for i in range(2)]
            gTb = [ctx_sb(actx, f"gTb{i}", [128, 512], BF16) for i in range(2)]
            ao = ctx_sb(actx, "ao", [128, D], F32)
            scr = ctx_sb(actx, "scr", [128, D], F32)
            junk = ctx_sb(actx, "junk", [128, D], F32)
            stats = [ctx_sb(actx, f"stats{i}", [128, 32], F32) for i in range(2)]

            ps_s = [ctx_ps(actx, f"ps_s{i}", [128, 512], F32) for i in range(2)]
            ps_t = [ctx_ps(actx, f"ps_t{i}", [128, 1024], BF16) for i in range(2)]
            ps_lt = [ctx_ps(actx, f"ps_lt{i}", [128, 1024], BF16) for i in range(2)]
            ps_o = ctx_ps(actx, "ps_o", [128, 1024], F32)

            kv3 = _re3(K_sb, k=S)
            kT3 = kT.rearrange("(c p) s -> p c s", p=128)
            t_kdma = []
            for j in range(8):
                sl = slice(j * 512, (j + 1) * 512)
                t_kdma.append(b.emit(
                    "pool",
                    lambda e, sl=sl: e.dma_start(out=kv3[:, :, sl], in_=kT3[:, :, sl]),
                    dma=True))
            t_vdma = []
            for g in range(8):
                src = vv[g * 512:(g + 1) * 512, :].rearrange("(c p) d -> p c d", p=128)
                dst = V_sb[:, g * 4096:(g + 1) * 4096].rearrange("p (c d) -> p c d", d=D)
                t_vdma.append(b.emit(
                    "pool", lambda e, src=src, dst=dst: e.dma_start(out=dst, in_=src),
                    dma=True))

            t_ld = {}

            def emit_load(t, dep):
                if t >= nt:
                    return
                p3 = t % 3
                t_ld[t] = b.emit(
                    "sq",
                    lambda e: e.dma_start(out=xb[p3], in_=xin[t * 128:(t + 1) * 128, :]),
                    deps=[dep], dma=True)

            emit_load(0, None)
            emit_load(1, None)
            emit_load(2, None)

            tok = {
                "ln1b_rd": [None, None],
                "ln1T_rd": [None, None],
                "pslt_rd": [None, None],
                "pss_rd": [None, None],
                "gbf_rd": [None, None],
                "pst_rd": [None, None],
                "gTb_rd": [None, None],
                "ao_wr": None,
                "ao_dve_rd": None,
                "tmp_wr": None,
                "x1add": {},
                "store": None,
                "mm2_last": None,
                "stats_rd_act": [None, None],
            }

            pdve = {}

            def pass_a_P_dve(t):
                p = t % 2
                p3 = t % 3
                stt = stats[p]
                d_x = t_ld[t]
                tk_r = b.emit("dve", lambda e: e.reduce_sum(
                    out=stt[:, 0:1], in_=xb[p3], axis=AX.X), deps=[d_x])
                tk_m = b.emit("dve", lambda e: e.tensor_scalar(
                    out=stt[:, 1:2], in0=stt[:, 0:1], scalar1=1.0 / D,
                    scalar2=None, op0=ALU.mult), deps=[tk_r])
                tk_v = b.emit("dve", lambda e: e.scalar_tensor_tensor(
                    out=junk, in0=xb[p3], scalar=stt[:, 1:2], in1=xb[p3],
                    op0=ALU.subtract, op1=ALU.mult, accum_out=stt[:, 2:3]),
                    deps=[tk_m])
                tk_ve = b.emit("dve", lambda e: e.tensor_scalar(
                    out=stt[:, 3:4], in0=stt[:, 2:3], scalar1=1.0 / D,
                    scalar2=EPS, op0=ALU.mult, op1=ALU.add), deps=[tk_v])
                tk_rs, rstd = rsqrt(stt, 4, stt[:, 3:4], tk_ve)
                tk_b = b.emit("dve", lambda e: e.scalar_tensor_tensor(
                    out=stt[:, 9:10], in0=stt[:, 1:2], scalar=-1.0, in1=rstd,
                    op0=ALU.mult, op1=ALU.mult), deps=[tk_rs])
                pdve[t] = (tk_b, rstd)

            def pass_a_P_act(t):
                p = t % 2
                p3 = t % 3
                stt = stats[p]
                tk_b, rstd = pdve[t]
                t_n = b.emit("act", lambda e: e.activation(
                    out=ln1b[p], in_=xb[p3], func=AF_COPY,
                    bias=stt[:, 9:10], scale=rstd),
                    deps=[tk_b, tok["ln1b_rd"][p], tok["stats_rd_act"][p]])
                tt = None
                for ci in range(8):
                    dl = [t_n, tok["pslt_rd"][p]] if ci == 0 else []
                    tt = b.emit("pe", lambda e, ci=ci: e.transpose(
                        ps_lt[p][:, ci * 128:(ci + 1) * 128],
                        ln1b[p][:, ci * 128:(ci + 1) * 128], ident),
                        deps=dl + ([t_id] if ci == 0 else []))
                tok["ln1b_rd"][p] = tt
                ev = b.emit("act", lambda e: e.copy(ln1T[p], ps_lt[p]),
                            deps=[tt, tok["ln1T_rd"][p]])
                tok["pslt_rd"][p] = ev
                return ev

            def pass_a_C(t, ev_ln1T):
                p = t % 2
                stt = stats[p]
                mm1 = {}
                gel = {}
                g2tok = {}
                tr = {}
                evg = {}
                mm2tok = None

                def do_mm1(j):
                    par = j % 2
                    tk = None
                    for d in range(8):
                        dl = []
                        if d == 0:
                            dl = [ev_ln1T, t_kdma[j], tok["pss_rd"][par]]
                        tk = b.emit("pe", lambda e, d=d, j=j, par=par: e.matmul(
                            ps_s[par],
                            ln1T[p][:, d * 128:(d + 1) * 128],
                            K_sb[:, d * S + j * 512: d * S + (j + 1) * 512],
                            start=(d == 0), stop=(d == 7)), deps=dl)
                    mm1[j] = tk

                def do_gelu(j):
                    par = j % 2
                    gel[j] = b.emit("act", lambda e, par=par: e.activation(
                        out=gbf[par], in_=ps_s[par], func=AF.Gelu),
                        deps=[mm1[j]] + list(tok["gbf_rd"][par] or ()))
                    tok["pss_rd"][par] = gel[j]

                def do_g2(j):
                    par = j % 2
                    g2tok[j] = b.emit("dve", lambda e, j=j, par=par: e.scalar_tensor_tensor(
                        out=junk[:, 0:512], in0=gbf[par], scalar=0.0,
                        in1=gbf[par], op0=ALU.bypass, op1=ALU.mult,
                        accum_out=stt[:, 21 + j:22 + j]), deps=[gel[j]])

                def do_tr(j):
                    par = j % 2
                    tk = None
                    for q in range(4):
                        dl = [gel[j], tok["pst_rd"][par]] if q == 0 else []
                        tk = b.emit("pe", lambda e, q=q, par=par: e.transpose(
                            ps_t[par][:, q * 128:(q + 1) * 128],
                            gbf[par][:, q * 128:(q + 1) * 128], ident), deps=dl)
                    tr[j] = tk

                def do_evg(j):
                    par = j % 2
                    evg[j] = b.emit("act", lambda e, par=par: e.copy(
                        gTb[par], ps_t[par][:, 0:512]),
                        deps=[tr[j], tok["gTb_rd"][par]])
                    tok["pst_rd"][par] = evg[j]

                def do_mm2(j):
                    nonlocal mm2tok
                    par = j % 2
                    tk = None
                    for q in range(4):
                        for h in range(2):
                            dl = []
                            if q == 0 and h == 0:
                                dl = [evg[j], t_vdma[j], tok["ao_wr"]]
                            st_flag = (j == 0 and q == 0)
                            sp_flag = (j == 7 and q == 3)
                            tk = b.emit("pe", lambda e, j=j, q=q, h=h, par=par,
                                        st_flag=st_flag, sp_flag=sp_flag: e.matmul(
                                ps_o[:, h * 512:(h + 1) * 512],
                                gTb[par][:, q * 128:(q + 1) * 128],
                                V_sb[:, (4 * j + q) * D + h * 512:(4 * j + q) * D + (h + 1) * 512],
                                start=st_flag, stop=sp_flag), deps=dl)
                    mm2[j] = tk
                    mm2tok = tk

                mm2 = {}
                do_mm1(0)
                do_gelu(0)
                do_mm1(1)
                do_gelu(1)
                if t + 1 < nt:
                    pass_a_P_dve(t + 1)
                do_g2(0)
                do_g2(1)
                if t > 0:
                    pass_a_E_rest(t - 1, g2_of[t - 1], ao_of[t - 1])
                    emit_load(t + 2, tok["x1add"].get(t - 1))
                for j in range(8):
                    do_tr(j)
                    do_evg(j)
                    tok["gbf_rd"][j % 2] = (tr[j], g2tok[j])
                    if j + 2 < 8:
                        do_mm1(j + 2)
                    do_mm2(j)
                    tok["gTb_rd"][j % 2] = mm2[j]
                    if j + 2 < 8:
                        do_gelu(j + 2)
                        do_g2(j + 2)
                tok["mm2_last"] = mm2tok
                tok["ln1T_rd"][p] = mm1[7]
                ev_next = pass_a_P_act(t + 1) if t + 1 < nt else None
                return g2tok[7], mm2tok, ev_next

            def pass_a_ao(t, mm2tok):
                t_ao = b.emit("act", lambda e: e.copy(ao, ps_o),
                              deps=[mm2tok, tok["ao_dve_rd"]])
                tok["ao_wr"] = t_ao
                return t_ao

            def pass_a_E_rest(t, g2last, t_ao):
                p = t % 2
                p3 = t % 3
                stt = stats[p]
                t_sg = b.emit("dve", lambda e: e.reduce_sum(
                    out=stt[:, 29:30], in_=stt[:, 21:29], axis=AX.X),
                    deps=[g2last])
                tk_r = b.emit("dve", lambda e: e.reduce_sum(
                    out=stt[:, 10:11], in_=ao, axis=AX.X), deps=[t_ao])
                tk_m = b.emit("dve", lambda e: e.tensor_scalar(
                    out=stt[:, 11:12], in0=stt[:, 10:11], scalar1=1.0 / D,
                    scalar2=None, op0=ALU.mult), deps=[tk_r])
                tk_v = b.emit("dve", lambda e: e.scalar_tensor_tensor(
                    out=junk, in0=ao, scalar=stt[:, 11:12], in1=ao,
                    op0=ALU.subtract, op1=ALU.mult, accum_out=stt[:, 12:13]),
                    deps=[tk_m])
                tok["ao_dve_rd"] = tk_v
                tk_vs = b.emit("dve", lambda e: e.tensor_scalar(
                    out=stt[:, 13:14], in0=stt[:, 12:13], scalar1=1.0 / D,
                    scalar2=None, op0=ALU.mult), deps=[tk_v])
                tk_ve = b.emit("dve", lambda e: e.scalar_tensor_tensor(
                    out=stt[:, 14:15], in0=stt[:, 29:30], scalar=EPS / S,
                    in1=stt[:, 13:14], op0=ALU.mult, op1=ALU.add),
                    deps=[tk_vs, t_sg])
                tk_rs, rstd = rsqrt(stt, 15, stt[:, 14:15], tk_ve)
                t_tmp = b.emit("dve", lambda e: e.tensor_scalar(
                    out=scr, in0=ao, scalar1=stt[:, 11:12], scalar2=rstd,
                    op0=ALU.subtract, op1=ALU.mult),
                    deps=[tk_rs, tok.get("x1add_prev_scr")])
                t_x1 = b.emit("dve", lambda e: e.tensor_tensor(
                    out=x1b, in0=xb[p3], in1=scr, op=ALU.add),
                    deps=[t_tmp, tok["store"]])
                tok["x1add_prev_scr"] = t_x1
                tok["x1add"][t] = t_x1
                tok["store"] = b.emit("sq", lambda e: e.dma_start(
                    out=x1s[t * 128:(t + 1) * 128, :], in_=x1b),
                    deps=[t_x1], dma=True)

            tok["x1add_prev_scr"] = None
            pass_a_P_dve(0)
            ev_next = pass_a_P_act(0)
            g2_of = {}
            ao_of = {}
            mm2_prev = None
            for t in range(nt):
                if t > 0:
                    ao_of[t - 1] = pass_a_ao(t - 1, mm2_prev)
                g2_of[t], mm2_prev, ev_next = pass_a_C(t, ev_next)
            ao_of[nt - 1] = pass_a_ao(nt - 1, mm2_prev)
            pass_a_E_rest(nt - 1, g2_of[nt - 1], ao_of[nt - 1])

            bar = b.snapshot()

        b.barrier(bar)

        with ExitStack() as bctx:
            def bsb(n, s, d):
                return bctx.enter_context(nc.sbuf_tensor(n, s, d)).ap()

            def bps(n, s, d):
                return bctx.enter_context(nc.psum_tensor(n, s, d)).ap()

            Wr_sb = bsb("Wr_sb", [128, 8 * D], BF16)
            Wk_sb = bsb("Wk_sb", [128, 8 * D], BF16)
            Wv_sb = bsb("Wv_sb", [128, 8 * D], BF16)
            y1b = [bsb(f"y1b{i}", [128, D], F32) for i in range(3)]
            ln2b = [bsb(f"ln2b{i}", [128, D], BF16) for i in range(2)]
            ln2T = [bsb(f"ln2T{i}", [128, D], BF16) for i in range(2)]
            sh = bsb("sh", [128, D], BF16)
            xrT = bsb("xrT", [128, D], BF16)
            xkT = bsb("xkT", [128, D], BF16)
            mixd = bsb("mixd", [128, D], F32)
            t_rb = bsb("t_rb", [128, D], F32)
            scrB = bsb("scrB", [128, D], F32)
            kctb = bsb("kctb", [128, D], BF16)
            kctT = bsb("kctT", [128, D], BF16)
            yb = bsb("yb", [128, D], F32)
            outb = bsb("outb", [128, D], F32)
            statsB = [bsb(f"statsB{i}", [128, 32], F32) for i in range(2)]
            prevc = [bsb(f"prevc{i}", [128, 8], BF16) for i in range(2)]

            ps_r = bps("ps_r", [128, 1024], F32)
            ps_k = bps("ps_k", [128, 1024], F32)
            ps_v = bps("ps_v", [128, 1024], F32)
            ps_t1 = bps("ps_tb1", [128, 1024], BF16)
            ps_t2 = bps("ps_tb2", [128, 1024], BF16)

            t_wdma = {"r": [], "k": [], "v": []}
            for nm, dram, dst in (("r", wrT, Wr_sb), ("k", wkT, Wk_sb), ("v", wvT, Wv_sb)):
                for h in range(2):
                    src = dram[h * 512:(h + 1) * 512, :].rearrange("(c p) d -> p c d", p=128)
                    dsv = dst[:, h * 4096:(h + 1) * 4096].rearrange("p (c d) -> p c d", d=D)
                    t_wdma[nm].append(b.emit(
                        "pool",
                        lambda e, src=src, dsv=dsv: e.dma_start(out=dsv, in_=src),
                        dma=True))
            t_pc0 = b.emit("pool", lambda e: e.memset(prevc[0], 0.0))
            t_pc1 = b.emit("pool", lambda e: e.memset(prevc[1], 0.0))

            t_yld = {}

            def emit_yload(t, dep):
                if t >= nt:
                    return
                p3 = t % 3
                t_yld[t] = b.emit(
                    "sq",
                    lambda e: e.dma_start(out=y1b[p3], in_=x1s[t * 128:(t + 1) * 128, :]),
                    deps=[dep], dma=True)

            emit_yload(0, None)
            emit_yload(1, None)

            tokB = {
                "ln2b_rd": [None, None],
                "ln2T_rd": [None, None],
                "pst1_rd": None,
                "pst2_rd": None,
                "sh_rd": None,
                "xr_rd": None, "xk_rd": None,
                "psr_rd": None, "psk_rd": None, "psv_rd": None,
                "tr_rd": None,
                "scrB_rd": None,
                "kctb_rd": None, "kctT_rd": None,
                "yb_rd": None,
                "outb_st": None,
                "y1_rd": {},
                "prevc_rd": [t_pc0, t_pc1],
                "prevc_wr": [t_pc0, t_pc1],
                "stats_rd_act": [None, None],
                "mixd_rd": None,
            }

            pdveB = {}

            def stage1_dve(t):
                p = t % 2
                p3 = t % 3
                stt = statsB[p]
                d_y = t_yld[t]
                tk_r = b.emit("dve", lambda e: e.reduce_sum(
                    out=stt[:, 0:1], in_=y1b[p3], axis=AX.X), deps=[d_y])
                tk_m = b.emit("dve", lambda e: e.tensor_scalar(
                    out=stt[:, 1:2], in0=stt[:, 0:1], scalar1=1.0 / D,
                    scalar2=None, op0=ALU.mult), deps=[tk_r])
                tk_v = b.emit("dve", lambda e: e.scalar_tensor_tensor(
                    out=mixd, in0=y1b[p3], scalar=stt[:, 1:2], in1=y1b[p3],
                    op0=ALU.subtract, op1=ALU.mult, accum_out=stt[:, 2:3]),
                    deps=[tk_m, tokB["mixd_rd"]])
                tokB["mixd_rd"] = tk_v
                tk_ve = b.emit("dve", lambda e: e.tensor_scalar(
                    out=stt[:, 3:4], in0=stt[:, 2:3], scalar1=1.0 / D,
                    scalar2=EPS, op0=ALU.mult, op1=ALU.add), deps=[tk_v])
                tk_rs, rstd = rsqrt(stt, 4, stt[:, 3:4], tk_ve)
                tk_b = b.emit("dve", lambda e: e.scalar_tensor_tensor(
                    out=stt[:, 9:10], in0=stt[:, 1:2], scalar=-1.0, in1=rstd,
                    op0=ALU.mult, op1=ALU.mult), deps=[tk_rs])
                pdveB[t] = (tk_b, rstd)

            def stage1_ln2b(t):
                p = t % 2
                p3 = t % 3
                stt = statsB[p]
                tk_b, rstd = pdveB[t]
                return b.emit("act", lambda e: e.activation(
                    out=ln2b[p], in_=y1b[p3], func=AF_COPY, bias=stt[:, 9:10],
                    scale=rstd), deps=[tk_b, tokB["ln2b_rd"][p], tokB["stats_rd_act"][p]])

            def stage1_tr(t, t_n):
                p = t % 2
                tt = None
                for ci in range(8):
                    dl = [t_n, tokB["pst1_rd"], t_id] if ci == 0 else []
                    tt = b.emit("pe", lambda e, ci=ci: e.transpose(
                        ps_t1[:, ci * 128:(ci + 1) * 128],
                        ln2b[p][:, ci * 128:(ci + 1) * 128], ident), deps=dl)
                tokB["ln2b_rd"][p] = tt
                ev = b.emit("act", lambda e: e.copy(ln2T[p], ps_t1),
                            deps=[tt, tokB["ln2T_rd"][p]])
                tokB["pst1_rd"] = ev
                l3 = _re3(ln2T[p])
                pcw = b.emit("dve", lambda e: e.tensor_copy(
                    out=_re3(prevc[p], k=1), in_=l3[:, :, 127:128]),
                    deps=[ev, tokB["prevc_rd"][p]])
                tokB["prevc_wr"][p] = pcw
                return ev

            def stage1_shmix(t, ev_ln2T):
                p = t % 2
                l3 = _re3(ln2T[p])
                s3 = _re3(sh)
                if t == 0:
                    c_bulk = b.emit("dve", lambda e: e.tensor_copy(
                        out=s3[:, :, 2:128], in_=l3[:, :, 1:127]),
                        deps=[ev_ln2T, tokB["sh_rd"]])
                    c_m = b.emit("dve", lambda e: e.tensor_scalar(
                        out=s3[:, :, 1:2], in0=l3[:, :, 0:1],
                        scalar1=msk_sb[:, 0:1], scalar2=None, op0=ALU.mult),
                        deps=[ev_ln2T, t_msk])
                    c_0 = b.emit("dve", lambda e: e.tensor_copy(
                        out=s3[:, :, 0:1], in_=_re3(prevc[1], k=1)),
                        deps=[tokB["prevc_wr"][1]])
                    tokB["prevc_rd"][1] = c_0
                    sh_deps = [c_bulk, c_m, c_0]
                else:
                    c_bulk = b.emit("dve", lambda e: e.tensor_copy(
                        out=s3[:, :, 1:128], in_=l3[:, :, 0:127]),
                        deps=[ev_ln2T, tokB["sh_rd"]])
                    pp = (t - 1) % 2
                    c_0 = b.emit("dve", lambda e: e.tensor_copy(
                        out=s3[:, :, 0:1], in_=_re3(prevc[pp], k=1)),
                        deps=[tokB["prevc_wr"][pp]])
                    tokB["prevc_rd"][pp] = c_0
                    sh_deps = [c_bulk, c_0]
                xr_toks = []
                xk_toks = []
                last_mix = None
                for ci in range(8):
                    cs = slice(ci * 128, (ci + 1) * 128)
                    dl = sh_deps + [tokB["xr_rd"], tokB["xk_rd"], tokB["mixd_rd"]] if ci == 0 else []
                    td = b.emit("dve", lambda e, cs=cs: e.tensor_tensor(
                        out=mixd[:, cs], in0=ln2T[p][:, cs], in1=sh[:, cs],
                        op=ALU.subtract), deps=dl)
                    tr_ = b.emit("dve", lambda e, cs=cs, ci=ci: e.scalar_tensor_tensor(
                        out=xrT[:, cs], in0=mixd[:, cs], scalar=mur_sb[:, ci:ci + 1],
                        in1=sh[:, cs], op0=ALU.mult, op1=ALU.add),
                        deps=[td] + ([t_mur] if ci == 0 else []))
                    tk_ = b.emit("dve", lambda e, cs=cs, ci=ci: e.scalar_tensor_tensor(
                        out=xkT[:, cs], in0=mixd[:, cs], scalar=muk_sb[:, ci:ci + 1],
                        in1=sh[:, cs], op0=ALU.mult, op1=ALU.add),
                        deps=[td] + ([t_muk] if ci == 0 else []))
                    xr_toks.append(tr_)
                    xk_toks.append(tk_)
                    last_mix = tk_
                tokB["sh_rd"] = last_mix
                tokB["ln2T_rd"][p] = last_mix
                tokB["mixd_rd"] = last_mix
                return xr_toks, xk_toks, last_mix

            def ffn_mm(ps, xT, W_sb, chunk_toks, wdmas, free_tok):
                tk = None
                for ci in range(8):
                    for h in range(2):
                        dl = [chunk_toks[ci], wdmas[ci // 4]]
                        if ci == 0 and h == 0:
                            dl.append(free_tok)
                        tk = b.emit("pe", lambda e, ci=ci, h=h, ps=ps, xT=xT, W_sb=W_sb: e.matmul(
                            ps[:, h * 512:(h + 1) * 512],
                            xT[:, ci * 128:(ci + 1) * 128],
                            W_sb[:, ci * D + h * 512: ci * D + (h + 1) * 512],
                            start=(ci == 0), stop=(ci == 7)), deps=dl)
                return tk

            def stage_eps(t, t_y):
                p = t % 2
                stt = statsB[p]
                tk_r = b.emit("dve", lambda e: e.reduce_sum(
                    out=stt[:, 10:11], in_=yb, axis=AX.X), deps=[t_y])
                tk_m = b.emit("dve", lambda e: e.tensor_scalar(
                    out=stt[:, 11:12], in0=stt[:, 10:11], scalar1=1.0 / D,
                    scalar2=None, op0=ALU.mult), deps=[tk_r])
                tk_v = b.emit("dve", lambda e: e.scalar_tensor_tensor(
                    out=mixd, in0=yb, scalar=stt[:, 11:12], in1=yb,
                    op0=ALU.subtract, op1=ALU.mult, accum_out=stt[:, 12:13]),
                    deps=[tk_m, tokB["mixd_rd"]])
                tokB["mixd_rd"] = tk_v
                tk_ve = b.emit("dve", lambda e: e.tensor_scalar(
                    out=stt[:, 13:14], in0=stt[:, 12:13], scalar1=1.0 / D,
                    scalar2=4 * EPS, op0=ALU.mult, op1=ALU.add), deps=[tk_v])
                tk_rs, rstd = rsqrt(stt, 14, stt[:, 13:14], tk_ve)
                t_tmp = b.emit("dve", lambda e: e.tensor_scalar(
                    out=scrB, in0=yb, scalar1=stt[:, 11:12], scalar2=rstd,
                    op0=ALU.subtract, op1=ALU.mult),
                    deps=[tk_rs, tokB["scrB_rd"]])
                tokB["stats_rd_act"][p] = t_tmp
                t_o = b.emit("dve", lambda e: e.tensor_tensor(
                    out=outb, in0=y1b[t % 3], in1=scrB, op=ALU.add),
                    deps=[t_tmp, tokB["outb_st"]])
                tokB["scrB_rd"] = t_o
                tokB["yb_rd"] = (tk_v, t_tmp)
                tokB["y1_rd"][t] = t_o
                vs = max(0, t * 128 - 1)
                ve = min(own - 1, t * 128 + 126)
                if ve >= vs:
                    rs = vs - (t * 128 - 1)
                    cnt = ve - vs + 1
                    tokB["outb_st"] = b.emit("sq", lambda e: e.dma_start(
                        out=out[vs:vs + cnt, :], in_=outb[rs:rs + cnt, :]),
                        deps=[t_o], dma=True)

            stage1_dve(0)
            stage1_dve(1)
            ev0 = stage1_tr(0, stage1_ln2b(0))
            mix_cur = stage1_shmix(0, ev0)
            for t in range(nt):
                p = t % 2
                emit_yload(t + 2, tokB["y1_rd"].get(t - 1))
                xr_toks, xk_toks, last_mix = mix_cur
                mr = ffn_mm(ps_r, xrT, Wr_sb, xr_toks, t_wdma["r"], tokB["psr_rd"])
                tokB["xr_rd"] = mr
                t_t = b.emit("act", lambda e: e.activation(
                    out=t_rb, in_=ps_r, func=AF.Tanh, scale=0.5),
                    deps=[mr, tokB["tr_rd"]])
                tokB["psr_rd"] = t_t
                ev_next = None
                if t + 1 < nt:
                    ev_next = stage1_tr(t + 1, stage1_ln2b(t + 1))
                mk = ffn_mm(ps_k, xkT, Wk_sb, xk_toks, t_wdma["k"], tokB["psk_rd"])
                tokB["xk_rd"] = mk
                t_kr = b.emit("act", lambda e: e.activation(
                    out=scrB, in_=ps_k, func=AF.Relu),
                    deps=[mk, tokB["scrB_rd"]])
                tokB["psk_rd"] = t_kr
                t_ka = b.emit("act", lambda e: e.activation(
                    out=kctb, in_=scrB, func=AF.Square),
                    deps=[t_kr, tokB["kctb_rd"]])
                tokB["scrB_rd"] = t_ka
                tt = None
                for ci in range(8):
                    dl = [t_ka, tokB["pst2_rd"]] if ci == 0 else []
                    tt = b.emit("pe", lambda e, ci=ci: e.transpose(
                        ps_t2[:, ci * 128:(ci + 1) * 128],
                        kctb[:, ci * 128:(ci + 1) * 128], ident), deps=dl)
                tokB["kctb_rd"] = tt
                ev2 = b.emit("act", lambda e: e.copy(kctT, ps_t2),
                             deps=[tt, tokB["kctT_rd"]])
                tokB["pst2_rd"] = ev2
                mv = ffn_mm(ps_v, kctT, Wv_sb, [ev2] * 8, t_wdma["v"], tokB["psv_rd"])
                tokB["kctT_rd"] = mv
                if t + 1 < nt:
                    mix_cur = stage1_shmix(t + 1, ev_next)
                if t + 2 < nt:
                    stage1_dve(t + 2)
                t_s = b.emit("dve", lambda e: e.tensor_tensor(
                    out=mixd, in0=t_rb, in1=ps_v, op=ALU.mult),
                    deps=[mv, t_t, tokB["mixd_rd"]])
                tokB["tr_rd"] = t_s
                tokB["mixd_rd"] = t_s
                t_y = b.emit("dve", lambda e: e.tensor_tensor(
                    out=yb, in0=mixd, in1=ps_v, op=ALU.add),
                    deps=[t_s] + list(tokB["yb_rd"] or ()))
                tokB["psv_rd"] = t_y
                stage_eps(t, t_y)

            if debug:
                b.emit("sq", lambda e: e.dma_start(
                    out=dbg["dbg_x1"], in_=x1s[0:256, :]), dma=True)
            sq = b.st["sq"]
            b.wait("sq", [(ds, ds.n) for ds in sq.dsems if ds.n])

    return nc


def _prep_inputs(x_flat, kT, v, wrT, wkT, wvT, mur, muk, nt=33):
    own = (nt - 1) * 128
    maps = []
    for c in range(NCORES):
        lo = c * own
        sl = x_flat[lo:lo + own]
        halo = x_flat[lo - 1:lo] if (lo % 8192) != 0 else sl[0:1]
        mask = np.ones((128, 1), np.float32) if (lo % 8192) != 0 else np.zeros((128, 1), np.float32)
        pad = sl[0:127]
        xin = np.ascontiguousarray(np.concatenate([halo, sl, pad], axis=0))
        maps.append({
            "xin": xin, "kT": kT, "vv": v, "wrT": wrT, "wkT": wkT,
            "wvT": wvT, "mur": mur, "muk": muk, "msk": mask,
        })
    return maps


def kernel(L0_attn_inputs, L0_attn_key_param_tokens, L0_attn_value_param_tokens,
           L0_ffn_W_r, L0_ffn_W_k, L0_ffn_W_v, L0_ffn_mu_r, L0_ffn_mu_k):
    x = np.ascontiguousarray(L0_attn_inputs, dtype=np.float32).reshape(-1, D)
    kT = np.ascontiguousarray(np.asarray(L0_attn_key_param_tokens, dtype=np.float32).T)
    v = np.ascontiguousarray(L0_attn_value_param_tokens, dtype=np.float32)
    wrT = np.ascontiguousarray(np.asarray(L0_ffn_W_r, dtype=np.float32).T)
    wkT = np.ascontiguousarray(np.asarray(L0_ffn_W_k, dtype=np.float32).T)
    wvT = np.ascontiguousarray(np.asarray(L0_ffn_W_v, dtype=np.float32).T)
    mur = np.ascontiguousarray(np.asarray(L0_ffn_mu_r, dtype=np.float32).reshape(8, 128).T)
    muk = np.ascontiguousarray(np.asarray(L0_ffn_mu_k, dtype=np.float32).reshape(8, 128).T)

    nt = 33
    nc = build_program(nt)
    in_maps = _prep_inputs(x, kT, v, wrT, wkT, wvT, mur, muk, nt)
    res = run_bass_kernel_spmd(nc, in_maps, core_ids=list(range(NCORES)))
    outs = [res.results[c]["out"] for c in range(NCORES)]
    full = np.concatenate(outs, axis=0).reshape(4, 8192, D)
    return full


# version 05
# speedup vs baseline: 1.4341x; 1.0917x over previous
import numpy as np
import concourse.bass as bass
import concourse.mybir as mybir
from concourse.bass_utils import run_bass_kernel_spmd
from contextlib import ExitStack

F32 = mybir.dt.float32
BF16 = mybir.dt.bfloat16
I32 = mybir.dt.int32
AF = mybir.ActivationFunctionType
ALU = mybir.AluOpType
AX = mybir.AxisListType

D = 1024
S = 4096
NCORES = 8
MAGIC = 0x5F3759DF
EPS = 1e-5
AF_COPY = AF.Identity


class _Sem:
    __slots__ = ("h", "step", "n")

    def __init__(self, h, step):
        self.h = h
        self.step = step
        self.n = 0


class _Stream:
    __slots__ = ("e", "csem", "dsems", "drot", "waited")

    def __init__(self, e, csem, dsems):
        self.e = e
        self.csem = csem
        self.dsems = dsems
        self.drot = 0
        self.waited = {}


class _B:

    def __init__(self, nc, ctx):
        self.nc = nc
        self.ctx = ctx
        self.st = {}
        for name, eng, n_d in (
            ("pe", nc.tensor, 0),
            ("act", nc.scalar, 0),
            ("dve", nc.vector, 0),
            ("pool", nc.gpsimd, 6),
            ("sq", nc.sync, 6),
        ):
            cs = _Sem(ctx.enter_context(nc.semaphore(name + "_c")), 1)
            ds = [_Sem(ctx.enter_context(nc.semaphore(f"{name}_d{i}")), 16)
                  for i in range(n_d)]
            self.st[name] = _Stream(eng, cs, ds)

    def emit(self, stname, fn, deps=(), dma=False):
        st = self.st[stname]
        for dep in deps:
            if dep is None:
                continue
            sem, cnt = dep
            if st.waited.get(id(sem), 0) >= cnt:
                continue
            st.e.wait_ge(sem.h, cnt * sem.step)
            st.waited[id(sem)] = cnt
        if dma:
            sem = st.dsems[st.drot]
            st.drot = (st.drot + 1) % len(st.dsems)
        else:
            sem = st.csem
        h = fn(st.e)
        h.then_inc(sem.h, sem.step)
        sem.n += 1
        return (sem, sem.n)

    def wait(self, stname, deps):
        st = self.st[stname]
        for dep in deps:
            if dep is None:
                continue
            sem, cnt = dep
            if st.waited.get(id(sem), 0) >= cnt:
                continue
            st.e.wait_ge(sem.h, cnt * sem.step)
            st.waited[id(sem)] = cnt

    def snapshot(self):
        toks = []
        for st in self.st.values():
            if st.csem.n:
                toks.append((st.csem, st.csem.n))
            for ds in st.dsems:
                if ds.n:
                    toks.append((ds, ds.n))
        return toks

    def barrier(self, toks):
        for name, st in self.st.items():
            own = {id(st.csem)} | {id(d) for d in st.dsems}
            for sem, cnt in toks:
                if id(sem) in own:
                    continue
                self.wait(name, [(sem, cnt)])


def _re3(ap, k=128):
    return ap.rearrange("p (c k) -> p c k", k=k)


def build_program(nt, debug=False):
    own = (nt - 1) * 128
    rows = nt * 128
    nc = bass.Bass("TRN2", target_bir_lowering=False)
    dbg = {}
    if debug:
        dbg["dbg_x1"] = nc.dram_tensor("dbg_x1", [256, D], F32, kind="ExternalOutput").ap()

    xin = nc.dram_tensor("xin", [rows, D], F32, kind="ExternalInput").ap()
    kT = nc.dram_tensor("kT", [D, S], F32, kind="ExternalInput").ap()
    vv = nc.dram_tensor("vv", [S, D], F32, kind="ExternalInput").ap()
    wrT = nc.dram_tensor("wrT", [D, D], F32, kind="ExternalInput").ap()
    wkT = nc.dram_tensor("wkT", [D, D], F32, kind="ExternalInput").ap()
    wvT = nc.dram_tensor("wvT", [D, D], F32, kind="ExternalInput").ap()
    mur = nc.dram_tensor("mur", [128, 8], F32, kind="ExternalInput").ap()
    muk = nc.dram_tensor("muk", [128, 8], F32, kind="ExternalInput").ap()
    msk = nc.dram_tensor("msk", [128, 1], F32, kind="ExternalInput").ap()
    out = nc.dram_tensor("out", [own, D], F32, kind="ExternalOutput").ap()
    x1s = nc.dram_tensor("x1s", [rows, D], F32, kind="Internal").ap()

    with ExitStack() as ctx:
        b = _B(nc, ctx)
        sb = lambda n, s, d: ctx.enter_context(nc.sbuf_tensor(n, s, d)).ap()
        pst = lambda n, s, d: ctx.enter_context(nc.psum_tensor(n, s, d)).ap()

        ident = sb("ident", [128, 128], BF16)
        magic = sb("magic", [128, 1], I32)
        mur_sb = sb("mur_sb", [128, 8], F32)
        muk_sb = sb("muk_sb", [128, 8], F32)
        msk_sb = sb("msk_sb", [128, 1], F32)

        t_id0 = b.emit("pool", lambda e: e.memset(ident, 0.0))
        t_id = b.emit(
            "pool",
            lambda e: e.affine_select(
                out=ident, in_=ident, compare_op=ALU.not_equal, fill=1.0,
                base=0, pattern=[[-1, 128]], channel_multiplier=1),
            deps=[t_id0],
        )
        t_magic = b.emit("pool", lambda e: e.memset(magic, MAGIC))
        t_mur = b.emit("sq", lambda e: e.dma_start(out=mur_sb, in_=mur), dma=True)
        t_muk = b.emit("sq", lambda e: e.dma_start(out=muk_sb, in_=muk), dma=True)
        t_msk = b.emit("sq", lambda e: e.dma_start(out=msk_sb, in_=msk), dma=True)

        def rsqrt(stt, base, v_ap, vtok):
            c0 = stt[:, base:base + 1]
            y0 = stt[:, base + 1:base + 2]
            t1 = stt[:, base + 2:base + 3]
            t2 = stt[:, base + 3:base + 4]
            yf = stt[:, base + 4:base + 5]
            tk = b.emit("dve", lambda e: e.tensor_scalar(
                out=c0.bitcast(I32), in0=v_ap.bitcast(I32), scalar1=1,
                scalar2=None, op0=ALU.logical_shift_right), deps=[vtok])
            tk = b.emit("pool", lambda e: e.tensor_tensor(
                out=y0.bitcast(I32), in0=magic, in1=c0.bitcast(I32),
                op=ALU.subtract), deps=[tk, t_magic])
            tk = b.emit("dve", lambda e: e.tensor_tensor(out=t1, in0=y0, in1=y0, op=ALU.mult), deps=[tk])
            tk = b.emit("dve", lambda e: e.tensor_tensor(out=t2, in0=t1, in1=v_ap, op=ALU.mult), deps=[tk])
            tk = b.emit("dve", lambda e: e.tensor_scalar(out=t1, in0=t2, scalar1=-0.5, scalar2=1.5, op0=ALU.mult, op1=ALU.add), deps=[tk])
            tk = b.emit("dve", lambda e: e.tensor_tensor(out=yf, in0=y0, in1=t1, op=ALU.mult), deps=[tk])
            tk = b.emit("dve", lambda e: e.tensor_tensor(out=t2, in0=yf, in1=yf, op=ALU.mult), deps=[tk])
            tk = b.emit("dve", lambda e: e.tensor_tensor(out=t1, in0=t2, in1=v_ap, op=ALU.mult), deps=[tk])
            tk = b.emit("dve", lambda e: e.tensor_scalar(out=t2, in0=t1, scalar1=-0.5, scalar2=1.5, op0=ALU.mult, op1=ALU.add), deps=[tk])
            tk = b.emit("dve", lambda e: e.tensor_tensor(out=y0, in0=yf, in1=t2, op=ALU.mult), deps=[tk])
            return tk, y0

        with ExitStack() as actx:
            asb = lambda n, s, d: ctx_sb(actx, n, s, d)

            def ctx_sb(c, n, s, d):
                return c.enter_context(nc.sbuf_tensor(n, s, d)).ap()

            def ctx_ps(c, n, s, d):
                return c.enter_context(nc.psum_tensor(n, s, d)).ap()

            K_sb = ctx_sb(actx, "K_sb", [128, 8 * S], BF16)
            V_sb = ctx_sb(actx, "V_sb", [128, 32 * D], BF16)
            xb = [ctx_sb(actx, f"xb{i}", [128, D], F32) for i in range(3)]
            x1b = ctx_sb(actx, "x1b", [128, D], F32)
            ln1b = [ctx_sb(actx, f"ln1b{i}", [128, D], BF16) for i in range(2)]
            ln1T = [ctx_sb(actx, f"ln1T{i}", [128, D], BF16) for i in range(2)]
            gbf = [ctx_sb(actx, f"gbf{i}", [128, 512], BF16) for i in range(2)]
            gTb = [ctx_sb(actx, f"gTb{i}", [128, 512], BF16) for i in range(2)]
            ao = ctx_sb(actx, "ao", [128, D], F32)
            scr = ctx_sb(actx, "scr", [128, D], F32)
            junk = ctx_sb(actx, "junk", [128, D], F32)
            stats = [ctx_sb(actx, f"stats{i}", [128, 32], F32) for i in range(2)]

            ps_s = [ctx_ps(actx, f"ps_s{i}", [128, 512], F32) for i in range(2)]
            ps_t = [ctx_ps(actx, f"ps_t{i}", [128, 1024], BF16) for i in range(2)]
            ps_lt = [ctx_ps(actx, f"ps_lt{i}", [128, 1024], BF16) for i in range(2)]
            ps_o = ctx_ps(actx, "ps_o", [128, 1024], F32)

            kv3 = _re3(K_sb, k=S)
            kT3 = kT.rearrange("(c p) s -> p c s", p=128)
            t_kdma = []
            for j in range(8):
                sl = slice(j * 512, (j + 1) * 512)
                t_kdma.append(b.emit(
                    "pool",
                    lambda e, sl=sl: e.dma_start(out=kv3[:, :, sl], in_=kT3[:, :, sl]),
                    dma=True))
            t_vdma = []
            for g in range(8):
                src = vv[g * 512:(g + 1) * 512, :].rearrange("(c p) d -> p c d", p=128)
                dst = V_sb[:, g * 4096:(g + 1) * 4096].rearrange("p (c d) -> p c d", d=D)
                t_vdma.append(b.emit(
                    "pool", lambda e, src=src, dst=dst: e.dma_start(out=dst, in_=src),
                    dma=True))

            t_ld = {}

            def emit_load(t, dep):
                if t >= nt:
                    return
                p3 = t % 3
                t_ld[t] = b.emit(
                    "sq",
                    lambda e: e.dma_start(out=xb[p3], in_=xin[t * 128:(t + 1) * 128, :]),
                    deps=[dep], dma=True)

            emit_load(0, None)
            emit_load(1, None)
            emit_load(2, None)

            tok = {
                "ln1b_rd": [None, None],
                "ln1T_rd": [None, None],
                "pslt_rd": [None, None],
                "pss_rd": [None, None],
                "gbf_rd": [None, None],
                "pst_rd": [None, None],
                "gTb_rd": [None, None],
                "ao_wr": None,
                "ao_dve_rd": None,
                "tmp_wr": None,
                "x1add": {},
                "store": None,
                "mm2_last": None,
                "stats_rd_act": [None, None],
            }

            pdve = {}

            def pass_a_P_dve(t):
                p = t % 2
                p3 = t % 3
                stt = stats[p]
                d_x = t_ld[t]
                tk_r = b.emit("dve", lambda e: e.reduce_sum(
                    out=stt[:, 0:1], in_=xb[p3], axis=AX.X), deps=[d_x])
                tk_m = b.emit("dve", lambda e: e.tensor_scalar(
                    out=stt[:, 1:2], in0=stt[:, 0:1], scalar1=1.0 / D,
                    scalar2=None, op0=ALU.mult), deps=[tk_r])
                tk_v = b.emit("dve", lambda e: e.scalar_tensor_tensor(
                    out=junk, in0=xb[p3], scalar=stt[:, 1:2], in1=xb[p3],
                    op0=ALU.subtract, op1=ALU.mult, accum_out=stt[:, 2:3]),
                    deps=[tk_m])
                tk_ve = b.emit("dve", lambda e: e.tensor_scalar(
                    out=stt[:, 3:4], in0=stt[:, 2:3], scalar1=1.0 / D,
                    scalar2=EPS, op0=ALU.mult, op1=ALU.add), deps=[tk_v])
                tk_rs, rstd = rsqrt(stt, 4, stt[:, 3:4], tk_ve)
                tk_b = b.emit("dve", lambda e: e.scalar_tensor_tensor(
                    out=stt[:, 9:10], in0=stt[:, 1:2], scalar=-1.0, in1=rstd,
                    op0=ALU.mult, op1=ALU.mult), deps=[tk_rs])
                pdve[t] = (tk_b, rstd)

            def pass_a_P_act(t):
                p = t % 2
                p3 = t % 3
                stt = stats[p]
                tk_b, rstd = pdve[t]
                t_n = b.emit("act", lambda e: e.activation(
                    out=ln1b[p], in_=xb[p3], func=AF_COPY,
                    bias=stt[:, 9:10], scale=rstd),
                    deps=[tk_b, tok["ln1b_rd"][p]])
                tt = None
                for ci in range(8):
                    dl = [t_n, tok["pslt_rd"][p]] if ci == 0 else []
                    tt = b.emit("pe", lambda e, ci=ci: e.transpose(
                        ps_lt[p][:, ci * 128:(ci + 1) * 128],
                        ln1b[p][:, ci * 128:(ci + 1) * 128], ident),
                        deps=dl + ([t_id] if ci == 0 else []))
                tok["ln1b_rd"][p] = tt
                ev = b.emit("act", lambda e: e.copy(ln1T[p], ps_lt[p]),
                            deps=[tt, tok["ln1T_rd"][p]])
                tok["pslt_rd"][p] = ev
                return ev

            def pass_a_C(t, ev_ln1T):
                p = t % 2
                stt = stats[p]
                mm1 = {}
                gel = {}
                g2tok = {}
                tr = {}
                evg = {}
                mm2tok = None

                def do_mm1(j):
                    par = j % 2
                    tk = None
                    for d in range(8):
                        dl = []
                        if d == 0:
                            dl = [ev_ln1T, t_kdma[j], tok["pss_rd"][par]]
                        tk = b.emit("pe", lambda e, d=d, j=j, par=par: e.matmul(
                            ps_s[par],
                            ln1T[p][:, d * 128:(d + 1) * 128],
                            K_sb[:, d * S + j * 512: d * S + (j + 1) * 512],
                            start=(d == 0), stop=(d == 7)), deps=dl)
                    mm1[j] = tk

                def do_gelu(j):
                    par = j % 2
                    gel[j] = b.emit("act", lambda e, par=par: e.activation(
                        out=gbf[par], in_=ps_s[par], func=AF.Gelu),
                        deps=[mm1[j]] + list(tok["gbf_rd"][par] or ()))
                    tok["pss_rd"][par] = gel[j]

                def do_g2(j):
                    par = j % 2
                    g2tok[j] = b.emit("dve", lambda e, j=j, par=par: e.scalar_tensor_tensor(
                        out=junk[:, 0:512], in0=gbf[par], scalar=0.0,
                        in1=gbf[par], op0=ALU.bypass, op1=ALU.mult,
                        accum_out=stt[:, 21 + j:22 + j]), deps=[gel[j]])

                def do_tr(j):
                    par = j % 2
                    tk = None
                    for q in range(4):
                        dl = [gel[j], tok["pst_rd"][par]] if q == 0 else []
                        tk = b.emit("pe", lambda e, q=q, par=par: e.transpose(
                            ps_t[par][:, q * 128:(q + 1) * 128],
                            gbf[par][:, q * 128:(q + 1) * 128], ident), deps=dl)
                    tr[j] = tk

                def do_evg(j):
                    par = j % 2
                    evg[j] = b.emit("act", lambda e, par=par: e.copy(
                        gTb[par], ps_t[par][:, 0:512]),
                        deps=[tr[j], tok["gTb_rd"][par]])
                    tok["pst_rd"][par] = evg[j]

                def do_mm2(j):
                    nonlocal mm2tok
                    par = j % 2
                    tk = None
                    for q in range(4):
                        for h in range(2):
                            dl = []
                            if q == 0 and h == 0:
                                dl = [evg[j], t_vdma[j], tok["ao_wr"]]
                            st_flag = (j == 0 and q == 0)
                            sp_flag = (j == 7 and q == 3)
                            tk = b.emit("pe", lambda e, j=j, q=q, h=h, par=par,
                                        st_flag=st_flag, sp_flag=sp_flag: e.matmul(
                                ps_o[:, h * 512:(h + 1) * 512],
                                gTb[par][:, q * 128:(q + 1) * 128],
                                V_sb[:, (4 * j + q) * D + h * 512:(4 * j + q) * D + (h + 1) * 512],
                                start=st_flag, stop=sp_flag), deps=dl)
                    mm2[j] = tk
                    mm2tok = tk

                mm2 = {}
                do_mm1(0)
                do_gelu(0)
                do_mm1(1)
                do_gelu(1)
                if t + 1 < nt:
                    pass_a_P_dve(t + 1)
                do_g2(0)
                do_g2(1)
                if t > 0:
                    pass_a_E_rest(t - 1, g2_of[t - 1], ao_of[t - 1])
                    emit_load(t + 2, tok["x1add"].get(t - 1))
                for j in range(8):
                    do_tr(j)
                    do_evg(j)
                    tok["gbf_rd"][j % 2] = (tr[j], g2tok[j])
                    if j + 2 < 8:
                        do_mm1(j + 2)
                    do_mm2(j)
                    tok["gTb_rd"][j % 2] = mm2[j]
                    if j + 2 < 8:
                        do_gelu(j + 2)
                        do_g2(j + 2)
                tok["mm2_last"] = mm2tok
                tok["ln1T_rd"][p] = mm1[7]
                ev_next = pass_a_P_act(t + 1) if t + 1 < nt else None
                return g2tok[7], mm2tok, ev_next

            def pass_a_ao(t, mm2tok):
                t_ao = b.emit("act", lambda e: e.copy(ao, ps_o),
                              deps=[mm2tok, tok["ao_dve_rd"]])
                tok["ao_wr"] = t_ao
                return t_ao

            def pass_a_E_rest(t, g2last, t_ao):
                p = t % 2
                p3 = t % 3
                stt = stats[p]
                t_sg = b.emit("dve", lambda e: e.reduce_sum(
                    out=stt[:, 29:30], in_=stt[:, 21:29], axis=AX.X),
                    deps=[g2last])
                tk_r = b.emit("dve", lambda e: e.reduce_sum(
                    out=stt[:, 10:11], in_=ao, axis=AX.X), deps=[t_ao])
                tk_m = b.emit("dve", lambda e: e.tensor_scalar(
                    out=stt[:, 11:12], in0=stt[:, 10:11], scalar1=1.0 / D,
                    scalar2=None, op0=ALU.mult), deps=[tk_r])
                tk_v = b.emit("dve", lambda e: e.scalar_tensor_tensor(
                    out=junk, in0=ao, scalar=stt[:, 11:12], in1=ao,
                    op0=ALU.subtract, op1=ALU.mult, accum_out=stt[:, 12:13]),
                    deps=[tk_m])
                tok["ao_dve_rd"] = tk_v
                tk_vs = b.emit("dve", lambda e: e.tensor_scalar(
                    out=stt[:, 13:14], in0=stt[:, 12:13], scalar1=1.0 / D,
                    scalar2=None, op0=ALU.mult), deps=[tk_v])
                tk_ve = b.emit("dve", lambda e: e.scalar_tensor_tensor(
                    out=stt[:, 14:15], in0=stt[:, 29:30], scalar=EPS / S,
                    in1=stt[:, 13:14], op0=ALU.mult, op1=ALU.add),
                    deps=[tk_vs, t_sg])
                tk_rs, rstd = rsqrt(stt, 15, stt[:, 14:15], tk_ve)
                t_tmp = b.emit("dve", lambda e: e.tensor_scalar(
                    out=scr, in0=ao, scalar1=stt[:, 11:12], scalar2=rstd,
                    op0=ALU.subtract, op1=ALU.mult),
                    deps=[tk_rs, tok.get("x1add_prev_scr")])
                t_x1 = b.emit("dve", lambda e: e.tensor_tensor(
                    out=x1b, in0=xb[p3], in1=scr, op=ALU.add),
                    deps=[t_tmp, tok["store"]])
                tok["x1add_prev_scr"] = t_x1
                tok["x1add"][t] = t_x1
                tok["store"] = b.emit("sq", lambda e: e.dma_start(
                    out=x1s[t * 128:(t + 1) * 128, :], in_=x1b),
                    deps=[t_x1], dma=True)

            tok["x1add_prev_scr"] = None
            pass_a_P_dve(0)
            ev_next = pass_a_P_act(0)
            g2_of = {}
            ao_of = {}
            mm2_prev = None
            for t in range(nt):
                if t > 0:
                    ao_of[t - 1] = pass_a_ao(t - 1, mm2_prev)
                g2_of[t], mm2_prev, ev_next = pass_a_C(t, ev_next)
            ao_of[nt - 1] = pass_a_ao(nt - 1, mm2_prev)
            pass_a_E_rest(nt - 1, g2_of[nt - 1], ao_of[nt - 1])

            bar = b.snapshot()

        b.barrier(bar)

        with ExitStack() as bctx:
            def bsb(n, s, d):
                return bctx.enter_context(nc.sbuf_tensor(n, s, d)).ap()

            def bps(n, s, d):
                return bctx.enter_context(nc.psum_tensor(n, s, d)).ap()

            Wr_sb = bsb("Wr_sb", [128, 8 * D], BF16)
            Wk_sb = bsb("Wk_sb", [128, 8 * D], BF16)
            Wv_sb = bsb("Wv_sb", [128, 8 * D], BF16)
            y1b = [bsb(f"y1b{i}", [128, D], F32) for i in range(3)]
            ln2b = [bsb(f"ln2b{i}", [128, D], BF16) for i in range(2)]
            ln2T = [bsb(f"ln2T{i}", [128, D], BF16) for i in range(2)]
            sh = bsb("sh", [128, D], BF16)
            xrT = bsb("xrT", [128, D], BF16)
            xkT = bsb("xkT", [128, D], BF16)
            mixd = bsb("mixd", [128, D], F32)
            t_rb = bsb("t_rb", [128, D], F32)
            scrB = bsb("scrB", [128, D], F32)
            kctb = bsb("kctb", [128, D], BF16)
            kctT = bsb("kctT", [128, D], BF16)
            yb = bsb("yb", [128, D], F32)
            outb = bsb("outb", [128, D], F32)
            statsB = [bsb(f"statsB{i}", [128, 32], F32) for i in range(2)]
            prevc = [bsb(f"prevc{i}", [128, 8], BF16) for i in range(2)]

            ps_r = bps("ps_r", [128, 1024], F32)
            ps_k = bps("ps_k", [128, 1024], F32)
            ps_v = bps("ps_v", [128, 1024], F32)
            ps_t1 = bps("ps_tb1", [128, 1024], BF16)
            ps_t2 = bps("ps_tb2", [128, 1024], BF16)

            t_wdma = {"r": [], "k": [], "v": []}
            for nm, dram, dst in (("r", wrT, Wr_sb), ("k", wkT, Wk_sb), ("v", wvT, Wv_sb)):
                for h in range(2):
                    src = dram[h * 512:(h + 1) * 512, :].rearrange("(c p) d -> p c d", p=128)
                    dsv = dst[:, h * 4096:(h + 1) * 4096].rearrange("p (c d) -> p c d", d=D)
                    t_wdma[nm].append(b.emit(
                        "pool",
                        lambda e, src=src, dsv=dsv: e.dma_start(out=dsv, in_=src),
                        dma=True))
            t_pc0 = b.emit("pool", lambda e: e.memset(prevc[0], 0.0))
            t_pc1 = b.emit("pool", lambda e: e.memset(prevc[1], 0.0))

            t_yld = {}

            def emit_yload(t, dep):
                if t >= nt:
                    return
                p3 = t % 3
                t_yld[t] = b.emit(
                    "sq",
                    lambda e: e.dma_start(out=y1b[p3], in_=x1s[t * 128:(t + 1) * 128, :]),
                    deps=[dep], dma=True)

            emit_yload(0, None)
            emit_yload(1, None)

            tokB = {
                "ln2b_rd": [None, None],
                "ln2T_rd": [None, None],
                "pst1_rd": None,
                "pst2_rd": None,
                "sh_rd": None,
                "xr_rd": None, "xk_rd": None,
                "psr_rd": None, "psk_rd": None, "psv_rd": None,
                "tr_rd": None,
                "scrB_rd": None,
                "kctb_rd": None, "kctT_rd": None,
                "yb_rd": None,
                "outb_st": None,
                "y1_rd": {},
                "prevc_rd": [t_pc0, t_pc1],
                "prevc_wr": [t_pc0, t_pc1],
                "stats_rd_act": [None, None],
                "mixd_rd": None,
            }

            pdveB = {}

            def stage1_dve(t):
                p = t % 2
                p3 = t % 3
                stt = statsB[p]
                d_y = t_yld[t]
                tk_r = b.emit("dve", lambda e: e.reduce_sum(
                    out=stt[:, 0:1], in_=y1b[p3], axis=AX.X), deps=[d_y])
                tk_m = b.emit("dve", lambda e: e.tensor_scalar(
                    out=stt[:, 1:2], in0=stt[:, 0:1], scalar1=1.0 / D,
                    scalar2=None, op0=ALU.mult), deps=[tk_r])
                tk_v = b.emit("dve", lambda e: e.scalar_tensor_tensor(
                    out=mixd, in0=y1b[p3], scalar=stt[:, 1:2], in1=y1b[p3],
                    op0=ALU.subtract, op1=ALU.mult, accum_out=stt[:, 2:3]),
                    deps=[tk_m, tokB["mixd_rd"]])
                tokB["mixd_rd"] = tk_v
                tk_ve = b.emit("dve", lambda e: e.tensor_scalar(
                    out=stt[:, 3:4], in0=stt[:, 2:3], scalar1=1.0 / D,
                    scalar2=EPS, op0=ALU.mult, op1=ALU.add), deps=[tk_v])
                tk_rs, rstd = rsqrt(stt, 4, stt[:, 3:4], tk_ve)
                tk_b = b.emit("dve", lambda e: e.scalar_tensor_tensor(
                    out=stt[:, 9:10], in0=stt[:, 1:2], scalar=-1.0, in1=rstd,
                    op0=ALU.mult, op1=ALU.mult), deps=[tk_rs])
                pdveB[t] = (tk_b, rstd)

            def stage1_ln2b(t):
                p = t % 2
                p3 = t % 3
                stt = statsB[p]
                tk_b, rstd = pdveB[t]
                return b.emit("act", lambda e: e.activation(
                    out=ln2b[p], in_=y1b[p3], func=AF_COPY, bias=stt[:, 9:10],
                    scale=rstd), deps=[tk_b, tokB["ln2b_rd"][p]])

            def stage1_tr(t, t_n):
                p = t % 2
                tt = None
                for ci in range(8):
                    dl = [t_n, tokB["pst1_rd"], t_id] if ci == 0 else []
                    tt = b.emit("pe", lambda e, ci=ci: e.transpose(
                        ps_t1[:, ci * 128:(ci + 1) * 128],
                        ln2b[p][:, ci * 128:(ci + 1) * 128], ident), deps=dl)
                tokB["ln2b_rd"][p] = tt
                ev = b.emit("act", lambda e: e.copy(ln2T[p], ps_t1),
                            deps=[tt, tokB["ln2T_rd"][p]])
                tokB["pst1_rd"] = ev
                l3 = _re3(ln2T[p])
                pcw = b.emit("dve", lambda e: e.tensor_copy(
                    out=_re3(prevc[p], k=1), in_=l3[:, :, 127:128]),
                    deps=[ev, tokB["prevc_rd"][p]])
                tokB["prevc_wr"][p] = pcw
                return ev

            def stage1_shmix(t, ev_ln2T):
                p = t % 2
                l3 = _re3(ln2T[p])
                s3 = _re3(sh)
                if t == 0:
                    c_bulk = b.emit("dve", lambda e: e.tensor_copy(
                        out=s3[:, :, 2:128], in_=l3[:, :, 1:127]),
                        deps=[ev_ln2T, tokB["sh_rd"]])
                    c_m = b.emit("dve", lambda e: e.tensor_scalar(
                        out=s3[:, :, 1:2], in0=l3[:, :, 0:1],
                        scalar1=msk_sb[:, 0:1], scalar2=None, op0=ALU.mult),
                        deps=[ev_ln2T, t_msk])
                    c_0 = b.emit("dve", lambda e: e.tensor_copy(
                        out=s3[:, :, 0:1], in_=_re3(prevc[1], k=1)),
                        deps=[tokB["prevc_wr"][1]])
                    tokB["prevc_rd"][1] = c_0
                    sh_deps = [c_bulk, c_m, c_0]
                else:
                    c_bulk = b.emit("dve", lambda e: e.tensor_copy(
                        out=s3[:, :, 1:128], in_=l3[:, :, 0:127]),
                        deps=[ev_ln2T, tokB["sh_rd"]])
                    pp = (t - 1) % 2
                    c_0 = b.emit("dve", lambda e: e.tensor_copy(
                        out=s3[:, :, 0:1], in_=_re3(prevc[pp], k=1)),
                        deps=[tokB["prevc_wr"][pp]])
                    tokB["prevc_rd"][pp] = c_0
                    sh_deps = [c_bulk, c_0]
                xr_toks = []
                xk_toks = []
                last_mix = None
                for ci in range(8):
                    cs = slice(ci * 128, (ci + 1) * 128)
                    dl = sh_deps + [tokB["xr_rd"], tokB["xk_rd"], tokB["mixd_rd"]] if ci == 0 else []
                    td = b.emit("dve", lambda e, cs=cs: e.tensor_tensor(
                        out=mixd[:, cs], in0=ln2T[p][:, cs], in1=sh[:, cs],
                        op=ALU.subtract), deps=dl)
                    tr_ = b.emit("dve", lambda e, cs=cs, ci=ci: e.scalar_tensor_tensor(
                        out=xrT[:, cs], in0=mixd[:, cs], scalar=mur_sb[:, ci:ci + 1],
                        in1=sh[:, cs], op0=ALU.mult, op1=ALU.add),
                        deps=[td] + ([t_mur] if ci == 0 else []))
                    tk_ = b.emit("dve", lambda e, cs=cs, ci=ci: e.scalar_tensor_tensor(
                        out=xkT[:, cs], in0=mixd[:, cs], scalar=muk_sb[:, ci:ci + 1],
                        in1=sh[:, cs], op0=ALU.mult, op1=ALU.add),
                        deps=[td] + ([t_muk] if ci == 0 else []))
                    xr_toks.append(tr_)
                    xk_toks.append(tk_)
                    last_mix = tk_
                tokB["sh_rd"] = last_mix
                tokB["ln2T_rd"][p] = last_mix
                tokB["mixd_rd"] = last_mix
                return xr_toks, xk_toks, last_mix

            def ffn_mm(ps, xT, W_sb, chunk_toks, wdmas, free_tok):
                tk = None
                for ci in range(8):
                    for h in range(2):
                        dl = [chunk_toks[ci], wdmas[ci // 4]]
                        if ci == 0 and h == 0:
                            dl.append(free_tok)
                        tk = b.emit("pe", lambda e, ci=ci, h=h, ps=ps, xT=xT, W_sb=W_sb: e.matmul(
                            ps[:, h * 512:(h + 1) * 512],
                            xT[:, ci * 128:(ci + 1) * 128],
                            W_sb[:, ci * D + h * 512: ci * D + (h + 1) * 512],
                            start=(ci == 0), stop=(ci == 7)), deps=dl)
                return tk

            def stage_eps(t, t_y):
                p = t % 2
                stt = statsB[p]
                tk_r = b.emit("dve", lambda e: e.reduce_sum(
                    out=stt[:, 10:11], in_=yb, axis=AX.X), deps=[t_y])
                tk_m = b.emit("dve", lambda e: e.tensor_scalar(
                    out=stt[:, 11:12], in0=stt[:, 10:11], scalar1=1.0 / D,
                    scalar2=None, op0=ALU.mult), deps=[tk_r])
                tk_v = b.emit("dve", lambda e: e.scalar_tensor_tensor(
                    out=mixd, in0=yb, scalar=stt[:, 11:12], in1=yb,
                    op0=ALU.subtract, op1=ALU.mult, accum_out=stt[:, 12:13]),
                    deps=[tk_m, tokB["mixd_rd"]])
                tokB["mixd_rd"] = tk_v
                tk_ve = b.emit("dve", lambda e: e.tensor_scalar(
                    out=stt[:, 13:14], in0=stt[:, 12:13], scalar1=1.0 / D,
                    scalar2=4 * EPS, op0=ALU.mult, op1=ALU.add), deps=[tk_v])
                tk_rs, rstd = rsqrt(stt, 14, stt[:, 13:14], tk_ve)
                t_tmp = b.emit("dve", lambda e: e.tensor_scalar(
                    out=scrB, in0=yb, scalar1=stt[:, 11:12], scalar2=rstd,
                    op0=ALU.subtract, op1=ALU.mult),
                    deps=[tk_rs, tokB["scrB_rd"]])
                tokB["stats_rd_act"][p] = t_tmp
                t_o = b.emit("dve", lambda e: e.tensor_tensor(
                    out=outb, in0=y1b[t % 3], in1=scrB, op=ALU.add),
                    deps=[t_tmp, tokB["outb_st"]])
                tokB["scrB_rd"] = t_o
                tokB["yb_rd"] = (tk_v, t_tmp)
                tokB["y1_rd"][t] = t_o
                vs = max(0, t * 128 - 1)
                ve = min(own - 1, t * 128 + 126)
                if ve >= vs:
                    rs = vs - (t * 128 - 1)
                    cnt = ve - vs + 1
                    tokB["outb_st"] = b.emit("sq", lambda e: e.dma_start(
                        out=out[vs:vs + cnt, :], in_=outb[rs:rs + cnt, :]),
                        deps=[t_o], dma=True)

            stage1_dve(0)
            stage1_dve(1)
            ev0 = stage1_tr(0, stage1_ln2b(0))
            mix_cur = stage1_shmix(0, ev0)
            for t in range(nt):
                p = t % 2
                emit_yload(t + 2, tokB["y1_rd"].get(t - 1))
                xr_toks, xk_toks, last_mix = mix_cur
                mr = ffn_mm(ps_r, xrT, Wr_sb, xr_toks, t_wdma["r"], tokB["psr_rd"])
                tokB["xr_rd"] = mr
                t_t = b.emit("act", lambda e: e.activation(
                    out=t_rb, in_=ps_r, func=AF.Tanh, scale=0.5),
                    deps=[mr, tokB["tr_rd"]])
                tokB["psr_rd"] = t_t
                ev_next = None
                if t + 1 < nt:
                    ev_next = stage1_tr(t + 1, stage1_ln2b(t + 1))
                mk = ffn_mm(ps_k, xkT, Wk_sb, xk_toks, t_wdma["k"], tokB["psk_rd"])
                tokB["xk_rd"] = mk
                t_kr = b.emit("act", lambda e: e.activation(
                    out=scrB, in_=ps_k, func=AF.Relu),
                    deps=[mk, tokB["scrB_rd"]])
                tokB["psk_rd"] = t_kr
                t_ka = b.emit("act", lambda e: e.activation(
                    out=kctb, in_=scrB, func=AF.Square),
                    deps=[t_kr, tokB["kctb_rd"]])
                tokB["scrB_rd"] = t_ka
                tt = None
                for ci in range(8):
                    dl = [t_ka, tokB["pst2_rd"]] if ci == 0 else []
                    tt = b.emit("pe", lambda e, ci=ci: e.transpose(
                        ps_t2[:, ci * 128:(ci + 1) * 128],
                        kctb[:, ci * 128:(ci + 1) * 128], ident), deps=dl)
                tokB["kctb_rd"] = tt
                ev2 = b.emit("act", lambda e: e.copy(kctT, ps_t2),
                             deps=[tt, tokB["kctT_rd"]])
                tokB["pst2_rd"] = ev2
                mv = ffn_mm(ps_v, kctT, Wv_sb, [ev2] * 8, t_wdma["v"], tokB["psv_rd"])
                tokB["kctT_rd"] = mv
                if t + 1 < nt:
                    mix_cur = stage1_shmix(t + 1, ev_next)
                if t + 2 < nt:
                    stage1_dve(t + 2)
                t_y = b.emit("dve", lambda e: e.scalar_tensor_tensor(
                    out=yb, in0=t_rb, scalar=1.0, in1=ps_v,
                    op0=ALU.add, op1=ALU.mult),
                    deps=[mv, t_t] + list(tokB["yb_rd"] or ()))
                tokB["tr_rd"] = t_y
                tokB["psv_rd"] = t_y
                stage_eps(t, t_y)

            if debug:
                b.emit("sq", lambda e: e.dma_start(
                    out=dbg["dbg_x1"], in_=x1s[0:256, :]), dma=True)
            sq = b.st["sq"]
            b.wait("sq", [(ds, ds.n) for ds in sq.dsems if ds.n])

    return nc


def _prep_inputs(x_flat, kT, v, wrT, wkT, wvT, mur, muk, nt=33):
    own = (nt - 1) * 128
    maps = []
    for c in range(NCORES):
        lo = c * own
        sl = x_flat[lo:lo + own]
        halo = x_flat[lo - 1:lo] if (lo % 8192) != 0 else sl[0:1]
        mask = np.ones((128, 1), np.float32) if (lo % 8192) != 0 else np.zeros((128, 1), np.float32)
        pad = sl[0:127]
        xin = np.ascontiguousarray(np.concatenate([halo, sl, pad], axis=0))
        maps.append({
            "xin": xin, "kT": kT, "vv": v, "wrT": wrT, "wkT": wkT,
            "wvT": wvT, "mur": mur, "muk": muk, "msk": mask,
        })
    return maps


def kernel(L0_attn_inputs, L0_attn_key_param_tokens, L0_attn_value_param_tokens,
           L0_ffn_W_r, L0_ffn_W_k, L0_ffn_W_v, L0_ffn_mu_r, L0_ffn_mu_k):
    x = np.ascontiguousarray(L0_attn_inputs, dtype=np.float32).reshape(-1, D)
    kT = np.ascontiguousarray(np.asarray(L0_attn_key_param_tokens, dtype=np.float32).T)
    v = np.ascontiguousarray(L0_attn_value_param_tokens, dtype=np.float32)
    wrT = np.ascontiguousarray(np.asarray(L0_ffn_W_r, dtype=np.float32).T)
    wkT = np.ascontiguousarray(np.asarray(L0_ffn_W_k, dtype=np.float32).T)
    wvT = np.ascontiguousarray(np.asarray(L0_ffn_W_v, dtype=np.float32).T)
    mur = np.ascontiguousarray(np.asarray(L0_ffn_mu_r, dtype=np.float32).reshape(8, 128).T)
    muk = np.ascontiguousarray(np.asarray(L0_ffn_mu_k, dtype=np.float32).reshape(8, 128).T)

    nt = 33
    nc = build_program(nt)
    in_maps = _prep_inputs(x, kT, v, wrT, wkT, wvT, mur, muk, nt)
    res = run_bass_kernel_spmd(nc, in_maps, core_ids=list(range(NCORES)))
    outs = [res.results[c]["out"] for c in range(NCORES)]
    full = np.concatenate(outs, axis=0).reshape(4, 8192, D)
    return full
